# version 22
# baseline (speedup 1.0000x reference)
import numpy as np
import ml_dtypes
import concourse.bass as bass
import concourse.mybir as mybir
from concourse.bass_utils import run_bass_kernel_spmd

F32 = mybir.dt.float32
BF16 = mybir.dt.bfloat16
AF = mybir.ActivationFunctionType
ALU = mybir.AluOpType
AX = mybir.AxisListType
bf16 = ml_dtypes.bfloat16

T = 4096
D = 1024
NCH, CW = 32, 128
AH = 16


class _Op:
    __slots__ = ("eng", "emit", "deps", "inc", "idx", "milestone", "tag",
                 "dma_sem", "dma_val")
    def __init__(self, eng, emit, idx, tag):
        self.eng, self.emit, self.idx, self.tag = eng, emit, idx, tag
        self.deps = set()
        self.inc = False
        self.milestone = None
        self.dma_sem = None
        self.dma_val = None


class Prog:
    DMA_ENGS = ("sp", "pool", "act", "dve")

    def __init__(self, nc):
        self.nc = nc
        self.engs = ("pe", "dve", "act", "sp", "pool")
        self.sems = {e: nc.alloc_semaphore(name=f"sem_{e}") for e in self.engs}
        self.ops = {e: [] for e in self.engs}
        self.last_writer = {}
        self.readers = {}
        self.tagctx = ""
        self.dma_sem_of = {}
        self.dma_counts = {}
        self.dma_last = {}

    def op(self, eng, emit, reads=(), writes=(), sreads=(), tag=""):
        o = _Op(eng, emit, len(self.ops[eng]), tag or self.tagctx)
        deps = set()
        force = set()
        for k in reads:
            w = self.last_writer.get(k)
            if w is not None:
                deps.add(w)
            self.readers.setdefault(k, []).append(o)
        for k in sreads:
            w = self.last_writer.get(k)
            if w is not None:
                deps.add(w)
                force.add(w)
            self.readers.setdefault(k, []).append(o)
        for k in writes:
            w = self.last_writer.get(k)
            if w is not None:
                deps.add(w)
            for rd in self.readers.get(k, ()):
                deps.add(rd)
            self.last_writer[k] = o
            self.readers[k] = []
        for d in deps:
            if d is o:
                continue
            if d.eng == eng == "pe" and d not in force:
                continue
            o.deps.add(d)
            d.inc = True
        self.ops[eng].append(o)
        return o

    def dma(self, eng, out, in_, reads=(), writes=(), tag="", group=None, **kw):
        assert eng in Prog.DMA_ENGS
        def emit(e, _o=out, _i=in_, _k=kw):
            return e.dma_start(out=_o, in_=_i, **_k)
        gk = (eng, group if group is not None else (writes[0] if writes else "dma_misc"))
        if gk not in self.dma_sem_of:
            name = f"dsem{len(self.dma_sem_of)}"
            self.dma_sem_of[gk] = (name, self.nc.alloc_semaphore(name=name))
            self.dma_counts[name] = 0
        sname, shandle = self.dma_sem_of[gk]
        o = self.op(eng, emit, reads=reads, writes=writes, tag=tag or self.tagctx)
        prev = self.dma_last.get(sname)
        if prev is not None:
            o.deps.add(prev)
            prev.inc = True
        o.inc = True
        o.dma_sem = (sname, shandle)
        self.dma_counts[sname] += 16
        o.dma_val = self.dma_counts[sname]
        self.dma_last[sname] = o
        return o

    def barrier(self):
        lasts = [o for o in (self.ops[e][-1] if self.ops[e] else None
                             for e in self.engs) if o is not None]
        lasts += [o for o in self.dma_last.values() if o is not None]
        for e in self.engs:
            if not lasts:
                continue
            o = self.op(e, None, tag="barrier")
            for d in lasts:
                if d is o:
                    continue
                o.deps.add(d)
                d.inc = True

    def finalize_and_emit(self):
        for e, ops in self.ops.items():
            cnt = 0
            for o in ops:
                if o.inc:
                    if o.dma_sem is not None:
                        o.milestone = (o.dma_sem[0], o.dma_val)
                    else:
                        cnt += 1
                        o.milestone = (e, cnt)
        all_waits = {}
        for e, ops in self.ops.items():
            ws = []
            for o in ops:
                w = {}
                for d in o.deps:
                    sn, v = d.milestone
                    w[sn] = max(w.get(sn, 0), v)
                ws.append(w)
            all_waits[e] = ws
        for e in self.engs:
            hi = {}
            for w in all_waits[e]:
                for se in list(w):
                    if hi.get(se, 0) >= w[se]:
                        del w[se]
                    else:
                        hi[se] = w[se]
        self._simulate(all_waits)
        finals = dict(self.dma_counts)
        for e, ops in self.ops.items():
            tot = sum(1 for o in ops if o.inc and o.dma_sem is None)
            if tot:
                finals[e] = tot
        prog = self
        semmap = dict(self.sems)
        for sn, sh in self.dma_sem_of.values():
            semmap[sn] = sh

        def mk_body(ename):
            def body(eng_obj):
                for o, w in zip(prog.ops[ename], all_waits[ename]):
                    for se, v in w.items():
                        eng_obj.wait_ge(semmap[se], v)
                    if o.emit is not None:
                        inst = o.emit(eng_obj)
                        if o.inc:
                            if o.dma_sem is not None:
                                inst.then_inc(o.dma_sem[1], 16)
                            else:
                                inst.then_inc(prog.sems[ename], 1)
                    elif o.inc:
                        eng_obj.nop().then_inc(prog.sems[ename], 1)
                if ename == "sp":
                    for se, v in finals.items():
                        eng_obj.wait_ge(semmap[se], v)
            return body

        with self.nc.Block() as blk:
            blk.tensor(mk_body("pe"))
            blk.vector(mk_body("dve"))
            blk.scalar(mk_body("act"))
            blk.sync(mk_body("sp"))
            blk.gpsimd(mk_body("pool"))

    def _simulate(self, all_waits):
        pc = {e: 0 for e in self.engs}
        val = {}
        n = {e: len(self.ops[e]) for e in self.engs}
        progress = True
        while progress:
            progress = False
            for e in self.engs:
                while pc[e] < n[e]:
                    w = all_waits[e][pc[e]]
                    if any(val.get(se, 0) < v for se, v in w.items()):
                        break
                    o = self.ops[e][pc[e]]
                    if o.inc:
                        sn, v = o.milestone
                        val[sn] = v
                    pc[e] += 1
                    progress = True
        stuck = {e: pc[e] for e in self.engs if pc[e] < n[e]}
        if stuck:
            lines = [f"  {e}@{pcv}/{n[e]} tag={self.ops[e][pcv].tag} "
                     f"waits={all_waits[e][pcv]}"
                     for e, pcv in stuck.items()]
            raise RuntimeError("DEADLOCK in program build:\n" + "\n".join(lines))



def _host_constants():
    c = {}
    c["ident_f32"] = np.eye(128, dtype=np.float32)
    c["ident_bf"] = np.eye(128, dtype=np.float32).astype(bf16)
    c["maskT"] = np.triu(np.ones((128, 128), dtype=np.float32))
    cols = np.arange(512)[None, :]
    rows = np.arange(128)[:, None]
    for pp in range(4):
        c[f"amask{pp}"] = (cols >= rows + 128 * pp).astype(np.float32)
    i = np.arange(32, dtype=np.float64)
    theta = 10000.0 ** (-2.0 * (i - 1.0) / 64)
    ang = np.arange(T, dtype=np.float64)[:, None] * theta[None, :]
    cos = np.cos(ang).astype(np.float32).T
    sin = np.sin(ang).astype(np.float32).T
    c["cos2"] = np.ascontiguousarray(np.concatenate([cos] * 4, axis=0))
    c["sin2"] = np.ascontiguousarray(np.concatenate([sin] * 4, axis=0))
    return c


def _chunked(wmat):
    K, O = wmat.shape
    assert K % 128 == 0
    return np.ascontiguousarray(
        wmat.reshape(K // 128, 128, O).transpose(1, 0, 2).reshape(128, (K // 128) * O))


def _prep_weights(inp):
    w = {}
    w["WinT"] = _chunked(np.ascontiguousarray(np.asarray(inp["L0_attn_in_proj_w"]).T))
    w["convw"] = np.ascontiguousarray(
        np.asarray(inp["L0_attn_conv_w"]).reshape(18, 128, 4).transpose(1, 0, 2).reshape(128, 72))
    w["convb"] = np.ascontiguousarray(np.asarray(inp["L0_attn_conv_b"]).reshape(18, 128).T)
    w["dtbias"] = np.asarray(inp["L0_attn_dt_bias"]).reshape(32, 1)
    w["Alog"] = np.asarray(inp["L0_attn_A_log"]).reshape(32, 1)
    w["Dcols"] = np.ascontiguousarray(
        np.asarray(inp["L0_attn_D"]).reshape(16, 2).repeat(64, axis=1).reshape(16, 128).T)
    w["normw"] = np.ascontiguousarray(np.asarray(inp["L0_attn_norm_w"]).reshape(16, 128).T)
    w["OPT"] = _chunked(np.ascontiguousarray(np.asarray(inp["L0_attn_out_proj_w"]).T))
    w["cnw1"] = _chunked(np.asarray(inp["L0_ffn_cn_w1"]))
    w["cnw2"] = _chunked(np.asarray(inp["L0_ffn_cn_w2"]))
    w["cnw3"] = _chunked(np.asarray(inp["L0_ffn_cn_w3"]))
    w["cnw4"] = _chunked(np.asarray(inp["L0_ffn_cn_w4"]))
    w["gcw1"] = _chunked(np.asarray(inp["L0_ffn_gc_w1"]))
    w["gcw2"] = _chunked(np.asarray(inp["L0_ffn_gc_w2"]))
    w["gcw3"] = _chunked(np.asarray(inp["L0_ffn_gc_w3"]))
    w["cnb1"] = np.ascontiguousarray(np.asarray(inp["L0_ffn_cn_b1"]).reshape(8, 128).T)
    w["cnb2"] = np.ascontiguousarray(np.asarray(inp["L0_ffn_cn_b2"]).reshape(4, 128).T)
    w["cnb3"] = np.ascontiguousarray(np.asarray(inp["L0_ffn_cn_b3"]).reshape(2, 128).T)
    w["gcb1"] = np.ascontiguousarray(np.asarray(inp["L0_ffn_gc_b1"]).reshape(4, 128).T)
    w["gcb2"] = np.ascontiguousarray(np.asarray(inp["L0_ffn_gc_b2"]).reshape(2, 128).T)
    w["rowc"] = np.asarray([[float(np.asarray(inp["L0_ffn_cn_b4"])[0]),
                             float(np.asarray(inp["L0_ffn_gc_b3"])[0]),
                             float(np.asarray(inp["L0_ffn_adaptive_range"])[0]),
                             float(np.asarray(inp["L1_ffn_scale"])),
                             0.0, 0.0, 0.0, 0.0]], dtype=np.float32)
    Wqkv = np.asarray(inp["L1_attn_Wqkv"])
    perm = np.arange(3 * D)
    half = np.concatenate([np.arange(0, 64, 2), np.arange(1, 64, 2)])
    for b in range(2 * AH):
        perm[b * 64:(b + 1) * 64] = b * 64 + half
    w["WqkvP"] = _chunked(np.ascontiguousarray(Wqkv[:, perm]))
    w["Wout"] = _chunked(np.asarray(inp["L1_attn_Wout"]))
    w["bout"] = np.asarray(inp["L1_attn_bout"]).reshape(1, 1024)
    w["W1T"] = _chunked(np.ascontiguousarray(np.asarray(inp["L1_ffn_W1"]).T))
    w["b1c"] = np.ascontiguousarray(np.asarray(inp["L1_ffn_b1"]).reshape(32, 128).T)
    w["W2T"] = _chunked(np.ascontiguousarray(np.asarray(inp["L1_ffn_W2"]).T))
    w["b2c"] = np.ascontiguousarray(np.asarray(inp["L1_ffn_b2"]).reshape(8, 128).T)
    return w


_W_SHAPES = {
    "WinT": (128, 8 * 4384), "convw": (128, 72), "convb": (128, 18),
    "dtbias": (32, 1), "Alog": (32, 1), "Dcols": (128, 16), "normw": (128, 16),
    "OPT": (128, 16 * 1024),
    "cnw1": (128, 8 * 1024), "cnw2": (128, 8 * 512), "cnw3": (128, 4 * 256),
    "cnw4": (128, 2), "gcw1": (128, 8 * 512), "gcw2": (128, 4 * 256),
    "gcw3": (128, 2),
    "cnb1": (128, 8), "cnb2": (128, 4), "cnb3": (128, 2),
    "gcb1": (128, 4), "gcb2": (128, 2), "rowc": (1, 8),
    "WqkvP": (128, 8 * 3072), "Wout": (128, 8 * 1024), "bout": (1, 1024),
    "W1T": (128, 8 * 4096), "b1c": (128, 32), "W2T": (128, 32 * 1024),
    "b2c": (128, 8),
}
_W_BF16 = {"WinT", "OPT", "cnw1", "cnw2", "cnw3", "cnw4", "gcw1", "gcw2", "gcw3",
           "WqkvP", "Wout", "W1T", "W2T"}
_C_SHAPES = {
    "ident_f32": (128, 128), "ident_bf": (128, 128), "maskT": (128, 128),
    "amask0": (128, 512), "amask1": (128, 512), "amask2": (128, 512),
    "amask3": (128, 512), "cos2": (128, T), "sin2": (128, T),
}



def build_program(upto=99, dbg=()):
    nc = bass.Bass("TRN2", target_bir_lowering=False)
    p = Prog(nc)
    CONST = "CONST"

    def din(name, shape, dt=F32):
        return nc.dram_tensor(name, list(shape), dt, kind="ExternalInput").ap()

    x_in = din("x", (T, D))
    hb = {k: din(k, s, BF16 if k == "ident_bf" else F32) for k, s in _C_SHAPES.items()}
    for k, s in _W_SHAPES.items():
        hb[k] = din(k, s, BF16 if k in _W_BF16 else F32)

    out_t = nc.dram_tensor("out", [T, D], F32, kind="ExternalOutput").ap()

    def scratch(name, shape, dt=F32):
        kind = "ExternalOutput" if name in dbg else "Internal"
        return nc.dram_tensor(name, list(shape), dt, kind=kind).ap()

    xnt_hbm = scratch("xnt_hbm", (1024, T), BF16)
    z_hbm = scratch("z_hbm", (2048, T), BF16)
    xssd_hbm = scratch("xssd_hbm", (2048, T), BF16)
    B_hbm = scratch("B_hbm", (128, T), BF16)
    C_hbm = scratch("C_hbm", (128, T), BF16)
    s123_hbm = scratch("s123_hbm", (96, T), BF16)
    m123_hbm = scratch("m123_hbm", (96, T), BF16)
    e12_hbm = scratch("e12_hbm", (64, T), BF16)
    g_hbm = scratch("g_hbm", (T, D), BF16)
    x1_hbm = scratch("x1_hbm", (T, D))
    q_hbm = scratch("q_hbm", (1024, T), BF16)
    k_hbm = scratch("k_hbm", (1024, T), BF16)
    qr_hbm = scratch("qr_hbm", (1024, T), BF16)
    kr_hbm = scratch("kr_hbm", (1024, T), BF16)
    v_hbm = scratch("v_hbm", (1024, T), BF16)
    rden_hbm = scratch("rden_hbm", (2 * AH, T), BF16)
    xrow_hbm = scratch("xrow_hbm", (128, D))
    a_hbm = scratch("a_hbm", (4, T))
    h_hbm = scratch("h_hbm", (4096, T), BF16)
    c_hbm = scratch("c_hbm", (4, 4 * T))
    e2_hbm = scratch("e2_hbm", (16, D))
    y_hbm = scratch("y_hbm", (1024, T), BF16)
    s1_hbm = scratch("s1_hbm", (T, D))
    ydbg_hbm = scratch("ydbg_hbm", (2048, T)) if "ydbg_hbm" in dbg else None
    i0dbg_hbm = scratch("i0dbg_hbm", (128, 4096)) if "i0dbg_hbm" in dbg else None
    gydbg_hbm = scratch("gydbg_hbm", (2048, T)) if "gydbg_hbm" in dbg else None

    sbal = lambda n, s, d=F32: nc.alloc_sbuf_tensor("sb_" + n, list(s), d).ap()
    ident = sbal("ident", (128, 128))
    identb = sbal("identb", (128, 128), BF16)
    maskT = sbal("maskT", (128, 128))
    misc = sbal("misc", (128, 8))
    dtb_col = sbal("dtb_col", (32, 1))
    nega_col = sbal("nega_col", (32, 1))
    dcols = sbal("dcols", (128, 16))
    normw = sbal("normw", (128, 16))
    convb = sbal("convb", (128, 18))
    convw = sbal("convw", (128, 72))
    onesrow = sbal("onesrow", (1, 128))
    onesb = sbal("onesb", (1, 128), BF16)
    onesbc = sbal("onesbc", (128, 1), BF16)
    ones2b = sbal("ones2b", (2, 128), BF16)
    rowc = sbal("rowc", (1, 8))
    rowm = sbal("rowm", (1, 8))
    AR32, AR16 = 22400, 46300
    ar32 = sbal("ar32", (128, AR32))
    ar16 = sbal("ar16", (128, AR16), BF16)
    pspair = [nc.alloc_psum_tensor(f"pp{i}", [128, 1024], F32).ap() for i in range(3)]
    psb = [pspair[i // 2][:, (i % 2) * 512:(i % 2 + 1) * 512] for i in range(6)]
    psbb = [nc.alloc_psum_tensor(f"pbb{i}", [128, 1024], BF16).ap() for i in range(2)]
    PS = [f"ps{i}" for i in range(6)]
    PSB = [f"psb{i}" for i in range(2)]

    class Arena:
        def __init__(self, ap, total, base=0):
            self.ap, self.total, self.off = ap, total, base
            self.marks = []
        def take(self, cols):
            assert self.off + cols <= self.total, (self.off, cols, self.total)
            v = self.ap[:, self.off:self.off + cols]
            self.off += cols
            return v
        def push(self):
            self.marks.append(self.off)
        def pop(self):
            self.off = self.marks.pop()

    dtde = ar32[:, 0:4096]
    hstate = ar32[:, 4096:6144]
    states_sb = ar32[:, 6144:8192]
    A32 = Arena(ar32, AR32, base=8192)
    A16 = Arena(ar16, AR16, base=0)

    def mm(ps, lhsT, rhs, reads, wkey, start=True, stop=True, tp=None, tag=""):
        def emit(e, _p=ps, _l=lhsT, _r=rhs, _s=start, _t=stop, _tp=tp):
            return e.matmul(_p, _l, _r, start=_s, stop=_t, tile_position=_tp,
                            skip_group_check=True)
        return p.op("pe", emit, reads=tuple(reads) + (wkey,), writes=(wkey,),
                    tag=tag or p.tagctx)

    def tr(ps, in_, idn, reads, wkey, tag=""):
        def emit(e, _p=ps, _i=in_, _n=idn):
            return e.transpose(_p, _i, _n)
        return p.op("pe", emit, reads=tuple(reads) + (wkey,), writes=(wkey,),
                    tag=tag or p.tagctx)

    def dve(emitfn, reads, writes, sreads=(), tag=""):
        return p.op("dve", emitfn, reads=reads, writes=writes, sreads=sreads,
                    tag=tag or p.tagctx)

    def act(emitfn, reads, writes, sreads=(), tag=""):
        return p.op("act", emitfn, reads=reads, writes=writes, sreads=sreads,
                    tag=tag or p.tagctx)

    eps_c = misc[:, 0:1]
    one_c = misc[:, 1:2]

    p.tagctx = "P0"
    for name, dst in (("ident_f32", ident), ("ident_bf", identb), ("maskT", maskT),
                      ("dtbias", dtb_col), ("Dcols", dcols), ("normw", normw),
                      ("convb", convb), ("convw", convw), ("rowc", rowc)):
        p.dma("sp", dst[:], hb[name], writes=(CONST,))
    dve(lambda e: e.memset(misc[:], 0.0), (), (CONST,))
    dve(lambda e: e.memset(misc[:, 0:1], 1e-5), (CONST,), (CONST,))
    dve(lambda e: e.memset(misc[:, 1:2], 1.0), (CONST,), (CONST,))
    dve(lambda e: e.memset(onesrow[:], 1.0), (), (CONST,))
    dve(lambda e: e.memset(onesb[:], 1.0), (), (CONST,))
    dve(lambda e: e.memset(onesbc[:], 1.0), (), (CONST,))
    dve(lambda e: e.memset(ones2b[:], 1.0), (), (CONST,))
    p.dma("sp", nega_col[:], hb["Alog"], writes=(CONST,))
    act(lambda e: e.activation(nega_col[:], nega_col[:], AF.Exp), (CONST,), (CONST,))
    mm(psb[0][:, 0:1], onesrow[:], rowc[0:1, 3:4], (CONST,), PS[0], tag="sclB")
    dve(lambda e: e.tensor_copy(misc[:, 2:3], psb[0][:, 0:1]), (PS[0], CONST),
        (CONST, PS[0]))
    p.barrier()

    def phase_LN(src_hbm, layer):
        A32.push(); A16.push()
        xt = [A32.take(1024), A32.take(1024)]
        cols = [A32.take(8), A32.take(8)]
        xsq = [A32.take(1024), A32.take(1024)]
        xnf = [A32.take(1024), A32.take(1024)]
        xnb = [A16.take(1024), A16.take(1024)]
        xstg = [A16.take(1024), A16.take(1024)]
        for i in range(32):
            bi = i % 2
            p.tagctx = f"L{layer}ln{i}"
            _xt, _xsq, _xnf, _xnb, _xstg = xt[bi], xsq[bi], xnf[bi], xnb[bi], xstg[bi]
            mean_c, var_c, tmp_c, sd_c, rstd_c = (cols[bi][:, j:j + 1] for j in range(5))
            KX, KC, KSQ, KF, KB = (f"lnxt{bi}", f"lnc{bi}", f"lnsq{bi}", f"lnf{bi}",
                                   f"lnb{bi}")
            KSs = [f"lnstg{bi}_{j}" for j in range(8)]
            p.dma("sp" if i % 2 == 0 else "act", _xt[:],
                  src_hbm[i * 128:(i + 1) * 128, :], writes=(KX,))
            dve(lambda e, _a=mean_c, _b=_xt: e.reduce_sum(_a, _b[:], axis=AX.X),
                (KX,), (KC,))
            act(lambda e, _a=_xsq, _b=_xt, _v=var_c: e.activation(
                _a[:], _b[:], AF.Square, accum_out=_v), (KX,), (KC, KSQ))
            dve(lambda e, _a=mean_c: e.tensor_scalar_mul(_a, _a, 1.0 / D), (KC,), (KC,))
            dve(lambda e, _a=var_c: e.tensor_scalar_mul(_a, _a, 1.0 / D), (KC,), (KC,))
            dve(lambda e, _a=sd_c, _b=mean_c: e.tensor_tensor(_a, _b, _b, ALU.mult),
                (KC,), (KC,))
            dve(lambda e, _a=tmp_c, _b=var_c, _c=sd_c: e.tensor_tensor(
                _a, _b, _c, ALU.subtract), (KC,), (KC,))
            act(lambda e, _a=sd_c, _b=tmp_c: e.activation(_a, _b, AF.Ln, bias=eps_c),
                (KC, CONST), (KC,))
            act(lambda e, _a=rstd_c, _b=sd_c: e.activation(_a, _b, AF.Exp, scale=-0.5),
                (KC,), (KC,))
            dve(lambda e, _t=tmp_c, _m=mean_c, _r=rstd_c: e.tensor_scalar(
                _t, _m, _r, -1.0, ALU.mult, ALU.mult), (KC,), (KC,))
            act(lambda e, _a=_xnb, _b=_xt, _r=rstd_c, _t=tmp_c: e.activation(
                _a[:], _b[:], AF.Identity, scale=_r, bias=_t), (KX,), (KB,),
                sreads=(KC,))
            if layer == 1 and i == 0:
                dve(lambda e, _a=_xnf, _b=_xt, _m=mean_c, _r=rstd_c: e.tensor_scalar(
                    _a[:], _b[:], _m, _r, ALU.subtract, ALU.mult), (KX,), (KF,),
                    sreads=(KC,))
                p.dma("pool", xrow_hbm, _xnf[:], reads=(KF,), writes=("xrow_hbm",))
            if layer == 0:
                for g2 in range(2):
                    mm(psb[4 + g2][0:1, :], onesbc[:], _xnb[:, g2 * 512:(g2 + 1) * 512],
                       (CONST, KB), PS[4 + g2], start=(i == 0), stop=(i == 31), tag="gcacc")
            for fc in range(8):
                pk = PSB[fc % 2]
                pb_ = psbb[fc % 2]
                tr(pb_[:, 0:128], _xnb[:, fc * 128:(fc + 1) * 128], identb[:],
                   (KB, CONST), pk)
                if fc % 2 == 0:
                    dve(lambda e, _f=fc, _pb=pb_, _s=_xstg: e.tensor_copy(
                        _s[:, _f * 128:(_f + 1) * 128], _pb[:, 0:128]),
                        (pk,), (KSs[fc], pk))
                else:
                    act(lambda e, _f=fc, _pb=pb_, _s=_xstg: e.copy(
                        _s[:, _f * 128:(_f + 1) * 128], _pb[:, 0:128]),
                        (pk,), (KSs[fc], pk))
            p.dma("pool", xnt_hbm[:, i * 128:(i + 1) * 128]
                  .rearrange("(c p) t -> p c t", p=128),
                  _xstg[:].rearrange("p (c t) -> p c t", c=8),
                  reads=tuple(KSs), writes=("xnt_hbm",))
        A32.pop(); A16.pop()

    p.tagctx = "wpre"
    A32.push(); A16.push()
    cw1 = A16.take(8 * 1024)
    cw2 = A16.take(8 * 512)
    cw3 = A16.take(4 * 256)
    cw4 = A16.take(2)
    gw1 = A16.take(8 * 512)
    gw2 = A16.take(4 * 256)
    gw3 = A16.take(2)
    cb1 = A32.take(8)
    cb2 = A32.take(4)
    cb3 = A32.take(2)
    gb1 = A32.take(4)
    gb2 = A32.take(2)
    p.dma("pool", cw1[:], hb["cnw1"], writes=("cw",))
    p.dma("act", cw2[:], hb["cnw2"], writes=("cw",))
    p.dma("act", cw3[:], hb["cnw3"], writes=("cw",))
    p.dma("act", cw4[:], hb["cnw4"], writes=("cw",))
    p.dma("act", gw1[:], hb["gcw1"], writes=("gw",))
    p.dma("act", gw2[:], hb["gcw2"], writes=("gw",))
    p.dma("act", gw3[:], hb["gcw3"], writes=("gw",))
    p.dma("act", cb1[:], hb["cnb1"], writes=("cw",))
    p.dma("act", cb2[:], hb["cnb2"], writes=("cw",))
    p.dma("act", cb3[:], hb["cnb3"], writes=("cw",))
    p.dma("act", gb1[:], hb["gcb1"], writes=("gw",))
    p.dma("act", gb2[:], hb["gcb2"], writes=("gw",))

    phase_LN(x_in, 0)
    p.barrier()

    p.tagctx = "gf"
    A32.push(); A16.push()
    gcb = A16.take(8)
    h1g = A16.take(4)
    h2g = A16.take(2)
    gcrow = A32.take(1024)
    gcf = A32.take(8)
    dve(lambda e: e.tensor_scalar_mul(gcrow[0:1, 0:512], psb[4][0:1, :], 1.0 / T),
        (PS[4],), ("gcrow", PS[4]))
    dve(lambda e: e.tensor_scalar_mul(gcrow[0:1, 512:1024], psb[5][0:1, :], 1.0 / T),
        (PS[5],), ("gcrow", PS[5]))
    for c8 in range(8):
        tr(psb[3][:, 0:1], gcrow[0:1, c8 * 128:(c8 + 1) * 128], ident[0:1, 0:1],
           ("gcrow", CONST), PS[3], tag="gctr")
        dve(lambda e, _c=c8: e.tensor_copy(gcf[:, _c:_c + 1], psb[3][:, 0:1]),
            (PS[3],), ("gcf", PS[3]))
    dve(lambda e: e.tensor_copy(gcb[:], gcf[:]), ("gcf",), ("gcb",))
    for j in range(4):
        for fc in range(8):
            mm(psb[0][:, j:j + 1], gw1[:, fc * 512 + j * 128:fc * 512 + (j + 1) * 128],
               gcb[:, fc:fc + 1], ("gw", "gcb"), PS[0],
               start=(j == 0 and fc == 0), stop=(fc == 7))
    for j in range(4):
        act(lambda e, _j=j: e.activation(h1g[:, _j:_j + 1], psb[0][:, _j:_j + 1],
                                         AF.Gelu, bias=gb1[:, _j:_j + 1]),
            (PS[0], "gw"), ("h1g",))
    for j in range(2):
        for fc in range(4):
            mm(psb[1][:, j:j + 1], gw2[:, fc * 256 + j * 128:fc * 256 + (j + 1) * 128],
               h1g[:, fc:fc + 1], ("gw", "h1g"), PS[1],
               start=(j == 0 and fc == 0), stop=(fc == 3))
    for j in range(2):
        act(lambda e, _j=j: e.activation(h2g[:, _j:_j + 1], psb[1][:, _j:_j + 1],
                                         AF.Gelu, bias=gb2[:, _j:_j + 1]),
            (PS[1], "gw"), ("h2g",))
    for fc in range(2):
        mm(psb[2][0:1, 0:1], gw3[:, fc:fc + 1], h2g[:, fc:fc + 1], ("gw", "h2g"), PS[2],
           start=(fc == 0), stop=(fc == 1))
    act(lambda e: e.activation(rowm[0:1, 0:1], psb[2][0:1, 0:1], AF.Sigmoid,
                               bias=rowc[0:1, 1:2]), (PS[2], CONST), ("rowm",))
    dve(lambda e: e.tensor_scalar_mul(rowm[0:1, 2:3], rowc[0:1, 2:3], 0.6),
        (CONST,), ("rowm",))
    dve(lambda e: e.tensor_scalar(rowm[0:1, 1:2], rowm[0:1, 0:1], rowc[0:1, 2:3], 1.4,
                                  ALU.mult, ALU.mult), (), ("rowm",),
        sreads=("rowm", CONST))
    dve(lambda e: e.tensor_scalar(rowm[0:1, 3:4], rowc[0:1, 2:3], -1.0, 0.1,
                                  ALU.mult, ALU.add), (), ("rowm",), sreads=(CONST,))
    dve(lambda e: e.tensor_tensor(rowm[0:1, 1:2], rowm[0:1, 1:2], rowm[0:1, 3:4],
                                  ALU.add), ("rowm",), ("rowm",))
    A32.pop(); A16.pop()
    p.barrier()
    if upto < 3:
        p.finalize_and_emit()
        return nc

    def load_xnt(buf, tg, key, eng="sp"):
        for fc in range(8):
            p.dma(eng, buf[:, fc * 512:(fc + 1) * 512],
                  xnt_hbm[fc * 128:(fc + 1) * 128, tg * 512:(tg + 1) * 512],
                  writes=(key,))

    p.tagctx = "ffn0"
    A32.push(); A16.push()
    BT = 512
    outA = A32.take(8 * BT)
    outB = A32.take(8 * BT)
    xsl = [A16.take(8 * BT), A16.take(8 * BT)]
    rhsb = A16.take(8 * BT)
    h1 = A16.take(8 * BT)
    h2 = A16.take(4 * BT)
    h3 = A16.take(2 * BT)
    tfrow = A32.take(BT)
    gstg = A16.take(1024)
    permq = [4, 5, 6, 7, 0, 1, 2, 3]
    signq = [1.0, 1.0, -1.0, -1.0, -1.0, -1.0, 1.0, 1.0]
    load_xnt(xsl[0], 0, "fxs0")
    for blk_i in range(8):
        p.tagctx = f"ffn0b{blk_i}"
        bi = blk_i % 2
        kxs = f"fxs{bi}"
        if blk_i + 1 < 8:
            load_xnt(xsl[1 - bi], blk_i + 1, f"fxs{1 - bi}")
        KOA = [f"ffA{j}" for j in range(8)]
        KOB = [f"ffB{j}" for j in range(8)]
        KOpair = [KOA, KOB]
        for fc in range(8):
            act(lambda e, _f=fc, _x=xsl[bi]: e.copy(
                outA[:, _f * BT:(_f + 1) * BT], _x[:, _f * BT:(_f + 1) * BT]),
                (kxs,), (KOA[fc],))
        cur, nxt = 0, 1
        for cyc in range(3):
            curks, nxtks = KOpair[cur], KOpair[nxt]
            curv = outA if cur == 0 else outB
            nxtv = outA if nxt == 0 else outB
            if cyc == 0:
                rhs_view = [xsl[bi][:, fc * BT:(fc + 1) * BT] for fc in range(8)]
                rhs_keys = [kxs] * 8
            else:
                for fc in range(8):
                    act(lambda e, _f=fc, _cv=curv: e.copy(
                        rhsb[:, _f * BT:(_f + 1) * BT], _cv[:, _f * BT:(_f + 1) * BT]),
                        (curks[fc],), (f"rhsb{fc}",))
                rhs_view = [rhsb[:, fc * BT:(fc + 1) * BT] for fc in range(8)]
                rhs_keys = [f"rhsb{fc}" for fc in range(8)]
            for oc in range(8):
                bank, bk = psb[oc % 4], PS[oc % 4]
                for fc in range(8):
                    mm(bank[:], cw1[:, fc * 1024 + oc * 128:fc * 1024 + (oc + 1) * 128],
                       rhs_view[fc], ("cw", rhs_keys[fc]), bk,
                       start=(fc == 0), stop=(fc == 7))
                act(lambda e, _o=oc, _b=bank: e.activation(
                    h1[:, _o * BT:(_o + 1) * BT], _b[:], AF.Gelu, bias=cb1[:, _o:_o + 1]),
                    (bk, "cw"), (f"h1_{oc}",))
            for oc in range(4):
                bank, bk = psb[oc % 4], PS[oc % 4]
                for fc in range(8):
                    mm(bank[:], cw2[:, fc * 512 + oc * 128:fc * 512 + (oc + 1) * 128],
                       h1[:, fc * BT:(fc + 1) * BT], ("cw", f"h1_{fc}"), bk,
                       start=(fc == 0), stop=(fc == 7))
                act(lambda e, _o=oc, _b=bank: e.activation(
                    h2[:, _o * BT:(_o + 1) * BT], _b[:], AF.Gelu, bias=cb2[:, _o:_o + 1]),
                    (bk, "cw"), (f"h2_{oc}",))
            for oc in range(2):
                bank, bk = psb[oc % 2], PS[oc % 2]
                for fc in range(4):
                    mm(bank[:], cw3[:, fc * 256 + oc * 128:fc * 256 + (oc + 1) * 128],
                       h2[:, fc * BT:(fc + 1) * BT], ("cw", f"h2_{fc}"), bk,
                       start=(fc == 0), stop=(fc == 3))
                act(lambda e, _o=oc, _b=bank: e.activation(
                    h3[:, _o * BT:(_o + 1) * BT], _b[:], AF.Gelu, bias=cb3[:, _o:_o + 1]),
                    (bk, "cw"), (f"h3_{oc}",))
            for fc in range(2):
                mm(psb[2][0:1, :], cw4[:, fc:fc + 1], h3[:, fc * BT:(fc + 1) * BT],
                   ("cw", f"h3_{fc}"), PS[2], start=(fc == 0), stop=(fc == 1))
            act(lambda e: e.activation(tfrow[0:1, :], psb[2][0:1, :], AF.Sigmoid,
                                       bias=rowc[0:1, 0:1]), (PS[2], CONST), ("tf",))
            dve(lambda e: e.tensor_scalar(tfrow[0:1, :], tfrow[0:1, :],
                                          rowm[0:1, 2:3], rowm[0:1, 1:2],
                                          ALU.mult, ALU.add), ("tf",), ("tf",),
                sreads=("rowm",))
            mm(psb[3][:], onesrow[:], tfrow[0:1, :], (CONST, "tf"), PS[3], tag="coupB")
            for fc in range(8):
                def up(e, _f=fc, _s=permq[fc], _sg=signq[fc], _cv=curv, _nv=nxtv):
                    return e.scalar_tensor_tensor(
                        _nv[:, _f * BT:(_f + 1) * BT], _cv[:, _s * BT:(_s + 1) * BT],
                        _sg, psb[3][:], ALU.mult, ALU.mult)
                dve(up, (curks[permq[fc]], PS[3]), (nxtks[fc], PS[3]))
                def up2(e, _f=fc, _cv=curv, _nv=nxtv):
                    return e.tensor_tensor(
                        _nv[:, _f * BT:(_f + 1) * BT], _nv[:, _f * BT:(_f + 1) * BT],
                        _cv[:, _f * BT:(_f + 1) * BT], ALU.add)
                dve(up2, (curks[fc], nxtks[fc]), (nxtks[fc],))
            cur, nxt = nxt, cur
        fin = outA if cur == 0 else outB
        finks = KOpair[cur]
        for fc in range(8):
            act(lambda e, _f=fc, _fv=fin: e.activation(
                _fv[:, _f * BT:(_f + 1) * BT], _fv[:, _f * BT:(_f + 1) * BT], AF.Silu),
                (finks[fc],), (finks[fc],))
        for ti in range(4):
            for fc in range(8):
                bank, bk = psb[4 + (fc % 2)], PS[4 + (fc % 2)]
                tr(bank[:, 0:128], fin[:, fc * BT + ti * 128:fc * BT + (ti + 1) * 128],
                   ident[:], (finks[fc], CONST), bk)
                act(lambda e, _f=fc, _b=bank: e.copy(
                    gstg[:, _f * 128:(_f + 1) * 128], _b[:, 0:128]), (bk,),
                    (f"gstg{fc}", bk))
            p.dma("pool", g_hbm[blk_i * BT + ti * 128:blk_i * BT + (ti + 1) * 128, :],
                  gstg[:], reads=tuple(f"gstg{j}" for j in range(8)), writes=("g_hbm",))
    A32.pop(); A16.pop()
    A32.pop(); A16.pop()
    p.barrier()
    if upto < 4:
        p.finalize_and_emit()
        return nc

    A16.push()
    opt_w = A16.take(16 * 1024)

    p.tagctx = "inproj"
    A32.push(); A16.push()
    winh = A16.take(8 * 2336)
    xsl4 = [A16.take(8 * 512), A16.take(8 * 512)]
    xcv = A32.take(18 * 516)
    zst = A16.take(512)
    cacc = A32.take(512)
    cacc2 = A32.take(512)
    sst = A16.take(512)
    dtp = A32.take(512)
    dve(lambda e: e.memset(xcv[:], 0.0), (), tuple(f"xcv{j}" for j in range(18)))
    for half in range(2):
        ocs = range(0, 16) if half == 0 else range(16, 35)
        off0 = 0 if half == 0 else 2048
        ncols = 2048 if half == 0 else 2336
        for fc in range(8):
            p.dma("pool", winh[:, fc * 2336:fc * 2336 + ncols],
                  hb["WinT"][:, fc * 4384 + off0:fc * 4384 + off0 + ncols],
                  writes=("winh",), tag="winld")
        if half == 0:
            p.dma("pool", opt_w[:], hb["OPT"], writes=("optw",), tag="optld")
        load_xnt(xsl4[0], 0, "ipx0")
        for tg in range(8):
            p.tagctx = f"inproj{half}_{tg}"
            bi = tg % 2
            kxs = f"ipx{bi}"
            if tg + 1 < 8:
                load_xnt(xsl4[1 - bi], tg + 1, f"ipx{1 - bi}")
            t0 = tg * 512
            for oc in ocs:
                bank, bk = psb[oc % 4], PS[oc % 4]
                M = 128 if oc < 34 else 32
                for fc in range(8):
                    mm(bank[0:M, :],
                       winh[:, fc * 2336 + (oc * 128 - off0):fc * 2336 + (oc * 128 - off0) + M],
                       xsl4[bi][:, fc * 512:(fc + 1) * 512], ("winh", kxs), bk,
                       start=(fc == 0), stop=(fc == 7))
                if oc < 16:
                    act(lambda e, _b=bank: e.copy(zst[:], _b[:]), (bk,), ("zst", bk))
                    p.dma("pool", z_hbm[oc * 128:(oc + 1) * 128, t0:t0 + 512], zst[:],
                          reads=("zst",), writes=("z_hbm",))
                elif oc < 34:
                    cc = oc - 16
                    xw_ = xcv[:, cc * 516:(cc + 1) * 516]
                    kcv = f"xcv{cc}"
                    dve(lambda e, _x=xw_: e.tensor_copy(_x[:, 0:3], _x[:, 512:515]),
                        (kcv,), (kcv,))
                    act(lambda e, _x=xw_, _b=bank: e.copy(_x[:, 3:515], _b[:]),
                        (bk, kcv), (kcv, bk))
                    dve(lambda e, _x=xw_, _c=cc: e.tensor_scalar(
                        cacc[:], _x[:, 0:512], convw[:, _c * 4:_c * 4 + 1], None, ALU.mult),
                        (kcv, CONST), ("cacc",))
                    dve(lambda e, _x=xw_, _c=cc: e.scalar_tensor_tensor(
                        cacc2[:], _x[:, 1:513], convw[:, _c * 4 + 1:_c * 4 + 2], cacc[:],
                        ALU.mult, ALU.add), (kcv, CONST, "cacc"), ("cacc2",))
                    dve(lambda e, _x=xw_, _c=cc: e.scalar_tensor_tensor(
                        cacc[:], _x[:, 2:514], convw[:, _c * 4 + 2:_c * 4 + 3], cacc2[:],
                        ALU.mult, ALU.add), (kcv, CONST, "cacc2"), ("cacc",))
                    dve(lambda e, _x=xw_, _c=cc: e.scalar_tensor_tensor(
                        cacc2[:], _x[:, 3:515], convw[:, _c * 4 + 3:_c * 4 + 4], cacc[:],
                        ALU.mult, ALU.add), (kcv, CONST, "cacc"), ("cacc2",))
                    act(lambda e, _c=cc: e.activation(sst[:], cacc2[:], AF.Silu,
                                                      bias=convb[:, _c:_c + 1]),
                        ("cacc2", CONST), ("sst",))
                    if cc < 16:
                        p.dma("pool", xssd_hbm[cc * 128:(cc + 1) * 128, t0:t0 + 512],
                              sst[:], reads=("sst",), writes=("xssd_hbm",))
                    elif cc == 16:
                        p.dma("pool", B_hbm[:, t0:t0 + 512], sst[:],
                              reads=("sst",), writes=("B_hbm",))
                    else:
                        p.dma("pool", C_hbm[:, t0:t0 + 512], sst[:],
                              reads=("sst",), writes=("C_hbm",))
                else:
                    dsl = dtde[0:32, t0:t0 + 512]
                    dve(lambda e, _b=bank, _d=dsl: e.tensor_scalar(
                        _d, _b[0:32, :], dtb_col[:], None, ALU.add), (bk, CONST), ("dtde", bk))
                    act(lambda e, _d=dsl: e.activation(dtp[0:32, :], _d, AF.Abs),
                        ("dtde",), ("dtp",))
                    act(lambda e: e.activation(dtp[0:32, :], dtp[0:32, :], AF.Exp, scale=-1.0),
                        ("dtp",), ("dtp",))
                    act(lambda e: e.activation(dtp[0:32, :], dtp[0:32, :], AF.Ln,
                                               bias=misc[0:32, 1:2]), ("dtp", CONST), ("dtp",))
                    dve(lambda e, _d=dsl: e.tensor_scalar_max(_d, _d, 0.0),
                        ("dtde",), ("dtde",))
                    dve(lambda e, _d=dsl: e.tensor_tensor(_d, _d, dtp[0:32, :], ALU.add),
                        ("dtde", "dtp"), ("dtde",))
    p.tagctx = "dtderived"
    p.barrier()
    A32.pop(); A32.push()
    A16.pop(); A16.push()
    a_sb = A32.take(T)
    acs_sb = A32.take(T)
    res_f = A32.take(T // 2)
    tmp_f = A32.take(T // 2)
    ones32 = A32.take(128)
    bufA = A16.take(T)
    bufB = A16.take(T)
    sb_s = [bufA[0:32, :], bufA[32:64, :], bufA[64:96, :]]
    sb_m = [bufB[0:32, :], bufB[32:64, :], bufB[64:96, :]]
    sb_e = [bufA[96:128, :], bufB[96:128, :]]
    dve(lambda e: e.memset(ones32[:], 1.0), (), ("ones32",))
    dve(lambda e: e.tensor_scalar(a_sb[0:32, :], dtde[0:32, :], nega_col[:], -1.0,
                                  ALU.mult, ALU.mult), ("dtde", CONST), ("a_sb",))
    for c in range(NCH):
        s0 = c * CW
        dve(lambda e, _s=s0: e.tensor_tensor_scan(
            acs_sb[0:32, _s:_s + CW], ones32[0:32, :], a_sb[0:32, _s:_s + CW], 0.0,
            ALU.mult, ALU.add), ("a_sb", "ones32"), ("acs_sb",))
    for c in range(NCH):
        s0 = c * CW
        act(lambda e, _s=s0: e.activation(
            dtde[32:64, _s:_s + CW], acs_sb[0:32, _s:_s + CW], AF.Exp,
            scale=-1.0, bias=acs_sb[0:32, _s + CW - 1:_s + CW]), ("acs_sb",), ("dtde",))
    dve(lambda e: e.tensor_scalar_mul(a_sb[0:32, :], acs_sb[0:32, :], -1.0),
        ("acs_sb",), ("a_sb",))

    H2 = T // 2
    for hh in range(2):
        sl = slice(hh * H2, (hh + 1) * H2)
        act(lambda e, _s=sl: e.copy(sb_s[0][0:32, _s], a_sb[0:32, _s]),
            ("a_sb",), ("spl",))
        act(lambda e, _s=sl: e.copy(tmp_f[0:32, 0:H2], sb_s[0][0:32, _s]),
            ("spl",), ("spt",))
        dve(lambda e, _s=sl: e.tensor_tensor(res_f[0:32, 0:H2], a_sb[0:32, _s],
                                             tmp_f[0:32, 0:H2], ALU.subtract),
            ("a_sb", "spt"), ("spr",))
        act(lambda e, _s=sl: e.copy(sb_s[1][0:32, _s], res_f[0:32, 0:H2]),
            ("spr",), ("spl",))
        act(lambda e, _s=sl: e.copy(tmp_f[0:32, 0:H2], sb_s[1][0:32, _s]),
            ("spl",), ("spt",))
        dve(lambda e: e.tensor_tensor(res_f[0:32, 0:H2], res_f[0:32, 0:H2],
                                      tmp_f[0:32, 0:H2], ALU.subtract),
            ("spr", "spt"), ("spr",))
        act(lambda e, _s=sl: e.copy(sb_s[2][0:32, _s], res_f[0:32, 0:H2]),
            ("spr",), ("spl",))
    for k in range(3):
        dve(lambda e, _k=k: e.tensor_scalar_mul(sb_m[_k][0:32, :], sb_s[_k][0:32, :],
                                                -1.0), ("spl",), ("spm",))
    act(lambda e: e.activation(a_sb[0:32, :], acs_sb[0:32, :], AF.Exp),
        ("acs_sb", "a_sb"), ("a_sb",))
    for hh in range(2):
        sl = slice(hh * H2, (hh + 1) * H2)
        act(lambda e, _s=sl: e.copy(sb_e[0][0:32, _s], a_sb[0:32, _s]),
            ("a_sb",), ("spl",))
        act(lambda e, _s=sl: e.copy(tmp_f[0:32, 0:H2], sb_e[0][0:32, _s]),
            ("spl",), ("spt",))
        dve(lambda e, _s=sl: e.tensor_tensor(res_f[0:32, 0:H2], a_sb[0:32, _s],
                                             tmp_f[0:32, 0:H2], ALU.subtract),
            ("a_sb", "spt"), ("spr",))
        act(lambda e, _s=sl: e.copy(sb_e[1][0:32, _s], res_f[0:32, 0:H2]),
            ("spr",), ("spl",))
    for k in range(3):
        p.dma("pool", s123_hbm[k * 32:(k + 1) * 32, :], sb_s[k][0:32, :],
              reads=("spl",), writes=("s123_hbm",))
        p.dma("pool", m123_hbm[k * 32:(k + 1) * 32, :], sb_m[k][0:32, :],
              reads=("spm",), writes=("m123_hbm",))
    for k in range(2):
        p.dma("pool", e12_hbm[k * 32:(k + 1) * 32, :], sb_e[k][0:32, :],
              reads=("spl",), writes=("e12_hbm",))
    A32.pop(); A16.pop()
    p.barrier()
    if upto < 5:
        p.finalize_and_emit()
        return nc

    p.tagctx = "ssd"
    A32.push(); A16.push()
    hstate_bf = A16.take(2048)
    xsdL = [A16.take(2048), A16.take(2048)]
    BL = [A16.take(128), A16.take(128)]
    CL = [A16.take(128), A16.take(128)]
    statb = A16.take(4096)
    movb = A16.take(4096)
    erows = A16.take(4096)
    zb = A32.take(2048)
    xgb = A32.take(2048)
    xtok = A16.take(2048)
    xdt = A16.take(2048)
    xw_t = A16.take(2048)
    btok = A16.take(128)
    mpair = A16.take(512)
    csc_all = A16.take(4096)
    ynb = xdt
    szb = xw_t
    cbt_sb = A32.take(128)
    dcst = A32.take(512)
    sctok = A32.take(64)
    y_sb = A32.take(2048)
    gy = A32.take(2048)
    sz = A32.take(2048)
    nrow = A32.take(128)
    x1st = A32.take(1024)
    dve(lambda e: e.memset(hstate[:], 0.0), (), ("hstate",), "hinit")

    def ssd_loads(c, bi):
        t0 = c * CW
        kx, kb, kc_ = f"xsd{bi}", f"Bb{bi}", f"Cb{bi}"
        p.dma("sp", xsdL[bi][:].rearrange("p (q t) -> p q t", q=16),
              xssd_hbm[:, t0:t0 + CW].rearrange("(q p) t -> p q t", p=128), writes=(kx,))
        p.dma("sp", BL[bi][:], B_hbm[:, t0:t0 + CW], writes=(kb,))
        p.dma("sp", CL[bi][:], C_hbm[:, t0:t0 + CW], writes=(kc_,))

    def flat_loads(c):
        t0 = c * CW
        for k in range(3):
            p.dma("sp", statb[k:k + 1, :].rearrange("p (h t) -> p h t", h=32),
                  s123_hbm[k * 32:(k + 1) * 32, t0:t0 + CW].unsqueeze(0), writes=("fl",))
            p.dma("sp", movb[3 + k:4 + k, :].rearrange("p (h t) -> p h t", h=32),
                  m123_hbm[k * 32:(k + 1) * 32, t0:t0 + CW].unsqueeze(0), writes=("fl",))
        for k in range(2):
            p.dma("sp", erows[k:k + 1, :].rearrange("p (h t) -> p h t", h=32),
                  e12_hbm[k * 32:(k + 1) * 32, t0:t0 + CW].unsqueeze(0), writes=("fl",))

    dve(lambda e: e.memset(statb[0:32, :], 1.0), (), ("fl",))
    dve(lambda e: e.memset(movb[0:32, :], 1.0), (), ("fl",))
    ssd_loads(0, 0)
    for cc in range(NCH):
        cb = cc % 2
        t0 = cc * CW
        p.tagctx = f"ssd{cc}"
        if cc + 1 < NCH:
            ssd_loads(cc + 1, 1 - cb)
        kx, kb, kc_, kfl = f"xsd{cb}", f"Bb{cb}", f"Cb{cb}", "fl"
        kz, kxg = "zb", "xgb"
        flat_loads(cc)
        p.dma("pool", zb[:].rearrange("p (q t) -> p q t", q=16),
              z_hbm[:, t0:t0 + CW].rearrange("(q p) t -> p q t", p=128), writes=(kz,))
        p.dma("pool", xgb[:, 0:1024], g_hbm[t0:t0 + CW, :], writes=(kxg,))
        p.dma("sp", xgb[:, 1024:2048], x_in[t0:t0 + CW, :], writes=(kxg,))
        def sctr(e, _t0=t0):
            return e.transpose(psb[5][0:128, 0:64], dtde[0:64, _t0:_t0 + CW],
                               ident[0:64, 0:64])
        p.op("pe", sctr, reads=("dtde", CONST, PS[5]), writes=(PS[5],), tag="sctr")
        act(lambda e: e.copy(sctok[:], psb[5][:, 0:64]), (PS[5],), ("sctok", PS[5]))
        tr(psbb[0][:, 0:128], BL[cb][:], identb[:], (kb, CONST), PSB[0])
        act(lambda e: e.copy(btok[:], psbb[0][:, 0:128]), (PSB[0],), ("btok", PSB[0]))
        for q in range(16):
            pk, pb_ = PSB[q % 2], psbb[q % 2]
            tr(pb_[:, 0:128], xsdL[cb][:, q * 128:(q + 1) * 128], identb[:], (kx, CONST), pk)
            act(lambda e, _q=q, _pb=pb_: e.copy(
                xtok[:, _q * 128:(_q + 1) * 128], _pb[:, 0:128]), (pk,),
                (f"xtok{q}", pk))
        dve(lambda e: e.tensor_tensor(
            xdt[:].rearrange("p (h j) -> p h j", h=32),
            xtok[:].rearrange("p (h j) -> p h j", h=32),
            sctok[:, 0:32].unsqueeze(2).broadcast_to([128, 32, 64]), ALU.mult),
            tuple(f"xtok{j}" for j in range(16)) + ("sctok",), ("xdt",))
        dve(lambda e: e.tensor_tensor(
            xw_t[:].rearrange("p (h j) -> p h j", h=32),
            xdt[:].rearrange("p (h j) -> p h j", h=32),
            sctok[:, 32:64].unsqueeze(2).broadcast_to([128, 32, 64]), ALU.mult),
            ("xdt", "sctok"), ("xw",))
        dve(lambda e: e.tensor_copy(hstate_bf[:], hstate[:]), ("hstate",), ("hbf",))
        mm(psb[3][:, 0:128], BL[cb][:], CL[cb][:], (kb, kc_), PS[3], tag="cbt")
        dve(lambda e: e.tensor_tensor(cbt_sb[:], psb[3][:, 0:128], maskT[:], ALU.mult),
            (PS[3], CONST), ("cbt", PS[3]))
        if cc > 0:
            for h4 in range(8):
                mm(psb[3][:], movb[0:2, 0:128],
                   erows[0:2, h4 * 512:(h4 + 1) * 512], (kfl,), PS[3], tag="eacsB")
                dve(lambda e, _h=h4, _c=CL[cb]: e.tensor_tensor(
                    csc_all[:, _h * 512:(_h + 1) * 512].rearrange("p (g t) -> p g t", g=4),
                    _c[:].unsqueeze(1).broadcast_to([128, 4, 128]),
                    psb[3][:].rearrange("p (g t) -> p g t", g=4),
                    ALU.mult), (kc_, PS[3]), (f"csca{h4}", PS[3]))
        def mchain(pr):
            par = pr % 2
            h0, h1_ = 2 * pr, 2 * pr + 1
            for hi, hh in enumerate((h0, h1_)):
                dbk = PS[2 + hi]
                mm(psb[2 + hi][:, 0:128], statb[0:6, hh * 128:(hh + 1) * 128],
                   movb[0:6, hh * 128:(hh + 1) * 128], (kfl,), dbk,
                   tag="diff")
                act(lambda e, _h=hi, _p=par, _b=psb[2 + hi]: e.activation(
                    dcst[:, (_p * 2 + _h) * 128:(_p * 2 + _h + 1) * 128], _b[:, 0:128],
                    AF.Relu, scale=-1.0), (dbk,), (f"dcst{par}{hi}", dbk))
                act(lambda e, _h=hi, _p=par: e.activation(
                    dcst[:, (_p * 2 + _h) * 128:(_p * 2 + _h + 1) * 128],
                    dcst[:, (_p * 2 + _h) * 128:(_p * 2 + _h + 1) * 128],
                    AF.Exp, scale=-1.0), (f"dcst{par}{hi}",), (f"dcst{par}{hi}",))
                dve(lambda e, _h=hi, _p=par: e.tensor_tensor(
                    mpair[:, (_p * 2 + _h) * 128:(_p * 2 + _h + 1) * 128],
                    dcst[:, (_p * 2 + _h) * 128:(_p * 2 + _h + 1) * 128],
                    cbt_sb[:], ALU.mult), (f"dcst{par}{hi}", "cbt"), (f"mpair{par}{hi}",))

        def ymms(pr):
            par = pr % 2
            h0, h1_ = 2 * pr, 2 * pr + 1
            ypk, ypb = PS[pr % 2], psb[pr % 2]
            mm(ypb[0:64, 0:128], xdt[:, h0 * 64:(h0 + 1) * 64],
               mpair[:, (par * 2) * 128:(par * 2 + 1) * 128],
               ("xdt", f"mpair{par}0"), ypk, start=True, stop=(cc == 0), tag="y1a")
            if cc > 0:
                mm(ypb[0:64, 0:128], hstate_bf[:, h0 * 64:(h0 + 1) * 64],
                   csc_all[:, h0 * 128:(h0 + 1) * 128],
                   ("hbf", f"csca{h0 // 4}"), ypk, start=False, stop=True, tag="y2a")
            mm(ypb[64:128, 0:128], xdt[:, h1_ * 64:(h1_ + 1) * 64],
               mpair[:, (par * 2 + 1) * 128:(par * 2 + 2) * 128],
               ("xdt", f"mpair{par}1"), ypk, start=True, stop=(cc == 0), tp=(0, 64),
               tag="y1b")
            if cc > 0:
                mm(ypb[64:128, 0:128], hstate_bf[:, h1_ * 64:(h1_ + 1) * 64],
                   csc_all[:, h1_ * 128:(h1_ + 1) * 128],
                   ("hbf", f"csca{h1_ // 4}"), ypk, start=False, stop=True,
                   tp=(0, 64), tag="y2b")
            spk, spb = PS[4 + (pr % 2)], psb[4 + (pr % 2)]
            mm(spb[:, 0:128], btok[:], xw_t[:, h0 * 64:(h0 + 2) * 64], ("btok", "xw"),
               spk, start=True, stop=True, tag="sta")
            act(lambda e, _p=pr, _b=spb: e.copy(
                states_sb[:, _p * 128:(_p + 1) * 128], _b[:, 0:128]), (spk,),
                (f"states{pr}", spk))
            def yev(e, _p=pr, _b=ypb, _x=xsdL[cb]):
                return e.scalar_tensor_tensor(
                    y_sb[:, _p * 128:(_p + 1) * 128], _x[:, _p * 128:(_p + 1) * 128],
                    dcols[:, _p:_p + 1], _b[:, 0:128], ALU.mult, ALU.add)
            dve(yev, (kx, CONST, ypk), (f"y_sb{pr}", ypk))

        mchain(0)
        for pr in range(16):
            if pr + 1 < 16:
                mchain(pr + 1)
            ymms(pr)
        mm(psb[2][:, 0:32], movb[0:2, 0:128], erows[0:2, 127:4096:128],
           (kfl,), PS[2], tag="cdecayB")
        dve(lambda e: e.tensor_tensor(
            hstate[:].rearrange("p (h j) -> p h j", h=32),
            hstate[:].rearrange("p (h j) -> p h j", h=32),
            psb[2][:, 0:32].unsqueeze(2).broadcast_to([128, 32, 64]), ALU.mult),
            ("hstate", PS[2]), ("hstate", PS[2]))
        dve(lambda e: e.tensor_tensor(hstate[:], hstate[:], states_sb[:], ALU.add),
            ("hstate",) + tuple(f"states{j}" for j in range(16)), ("hstate",))
        if ydbg_hbm is not None:
            p.dma("pool", ydbg_hbm[:, t0:t0 + CW].rearrange("(q p) t -> p q t", p=128),
                  y_sb[:].rearrange("p (q t) -> p q t", q=16),
                  reads=tuple(f"y_sb{j}" for j in range(16)), writes=("ydbg",))
        act(lambda e: e.activation(sz[:], zb[:], AF.Silu), (kz,), ("sz",))
        dve(lambda e: e.tensor_tensor(gy[:], y_sb[:], sz[:], ALU.mult),
            tuple(f"y_sb{j}" for j in range(16)) + ("sz",), ("gy",))
        act(lambda e: e.activation(szb[:], gy[:], AF.Square), ("gy", "xw"), ("xw",))
        for q in range(16):
            mm(psb[3][0:1, 0:128], onesbc[:], szb[:, q * 128:(q + 1) * 128],
               (CONST, "xw"), PS[3], start=(q == 0), stop=(q == 15), tag="nrm")
        dve(lambda e: e.tensor_scalar(nrow[0:1, :], psb[3][0:1, 0:128], 1.0 / 2048,
                                      misc[0:1, 0:1], ALU.mult, ALU.add),
            (PS[3],), ("nrow", PS[3]), sreads=(CONST,))
        act(lambda e: e.activation(nrow[0:1, :], nrow[0:1, :], AF.Ln), ("nrow",), ("nrow",))
        act(lambda e: e.activation(nrow[0:1, :], nrow[0:1, :], AF.Exp, scale=-0.5),
            ("nrow",), ("nrow",))
        mm(psb[3][:, 0:128], onesrow[:], nrow[0:1, :], ("nrow",), PS[3], tag="rstdB")
        dve(lambda e: e.tensor_tensor(
            gy[:].rearrange("p (q t) -> p q t", q=16),
            gy[:].rearrange("p (q t) -> p q t", q=16),
            psb[3][:, 0:128].unsqueeze(1).broadcast_to([128, 16, 128]), ALU.mult),
            ("gy", PS[3]), ("gy", PS[3]))
        dve(lambda e: e.tensor_tensor(
            ynb[:].rearrange("p (q t) -> p q t", q=16),
            gy[:].rearrange("p (q t) -> p q t", q=16),
            normw[:].unsqueeze(2).broadcast_to([128, 16, 128]), ALU.mult),
            ("gy", CONST), ("xdt",))
        if gydbg_hbm is not None:
            p.dma("pool", gydbg_hbm[:, t0:t0 + CW].rearrange("(q p) t -> p q t", p=128),
                  ynb[:].rearrange("p (q t) -> p q t", q=16),
                  reads=("ynb",), writes=("gydbg",))
        for og in range(2):
            bank, bk = psb[4 + og], PS[4 + og]
            for q in range(16):
                mm(bank[:], ynb[:, q * 128:(q + 1) * 128],
                   opt_w[:, q * 1024 + og * 512:q * 1024 + (og + 1) * 512],
                   ("xdt", "optw"), bk, start=(q == 0), stop=(q == 15))
            def comb1(e, _b=bank, _og=og):
                return e.tensor_tensor(x1st[:, _og * 512:(_og + 1) * 512], _b[:],
                                       xgb[:, _og * 512:(_og + 1) * 512], ALU.mult)
            dve(comb1, (bk, kxg), ("x1st", bk))
            def comb2(e, _og=og):
                return e.tensor_tensor(x1st[:, _og * 512:(_og + 1) * 512],
                                       x1st[:, _og * 512:(_og + 1) * 512],
                                       xgb[:, 1024 + _og * 512:1024 + (_og + 1) * 512],
                                       ALU.add)
            dve(comb2, ("x1st", kxg), ("x1st",))
        p.dma("pool", x1_hbm[t0:t0 + CW, :], x1st[:], reads=("x1st",), writes=("x1_hbm",))
    A32.pop(); A16.pop()
    A16.pop()
    p.barrier()
    if upto < 6:
        p.finalize_and_emit()
        return nc

    phase_LN(x1_hbm, 1)
    p.barrier()
    if upto < 7:
        p.finalize_and_emit()
        return nc

    p.tagctx = "qkv"
    A32.push(); A16.push()
    wq = A16.take(8 * 1536)
    cos2 = A32.take(4096)
    sin2 = A32.take(4096)
    p.dma("sp", cos2[:], hb["cos2"], writes=("cs",))
    p.dma("sp", sin2[:], hb["sin2"], writes=("cs",))
    xsl7 = [A16.take(8 * 512), A16.take(8 * 512)]
    tm1b = [A32.take(512), A32.take(512)]
    obfb = [A16.take(512), A16.take(512)]
    rbfb = [A16.take(512), A16.take(512)]
    for qhalf in range(2):
      for fc in range(8):
          p.dma("pool", wq[:, fc * 1536:(fc + 1) * 1536],
                hb["WqkvP"][:, fc * 3072 + qhalf * 1536:fc * 3072 + (qhalf + 1) * 1536],
                writes=("wq",))
      load_xnt(xsl7[0], 0, "qx0")
      for tg in range(8):
        p.tagctx = f"qkv{qhalf}_{tg}"
        bi = tg % 2
        kxs = f"qx{bi}"
        if tg + 1 < 8:
            load_xnt(xsl7[1 - bi], tg + 1, f"qx{1 - bi}")
        for oci in range(12):
            oc = qhalf * 12 + oci
            bank, bk = psb[oc % 4], PS[oc % 4]
            for fc in range(8):
                mm(bank[:], wq[:, fc * 1536 + oci * 128:fc * 1536 + (oci + 1) * 128],
                   xsl7[bi][:, fc * 512:(fc + 1) * 512], ("wq", kxs), bk,
                   start=(fc == 0), stop=(fc == 7))
            par = oc % 2
            obf, kob = obfb[par], f"obf{par}"
            if oc < 16:
                tm1, ktm = tm1b[par], f"tm1{par}"
                tm2, ktm2 = psb[4 + par], PS[4 + par]
                rbf, krb = rbfb[par], f"rbf{par}"
                act(lambda e, _b=bank, _o=obf: e.copy(_o[:], _b[:]), (bk,), (kob, bk))
                dve(lambda e, _b=bank, _t=tg, _m=tm1: e.tensor_tensor(
                    _m[:], _b[:], cos2[:, _t * 512:(_t + 1) * 512], ALU.mult),
                    (bk, "cs"), (ktm, bk))
                dve(lambda e, _b=bank, _t=tg, _m=tm2: e.tensor_tensor(
                    _m[:], _b[:], sin2[:, _t * 512:(_t + 1) * 512], ALU.mult),
                    (bk, "cs", ktm2), (ktm2, bk))
                for bb in (0, 64):
                    dve(lambda e, _b=bb, _m=tm1, _m2=tm2, _r=rbf: e.tensor_tensor(
                        _r[_b:_b + 32, :], _m[_b:_b + 32, :], _m2[_b + 32:_b + 64, :],
                        ALU.subtract), (ktm, ktm2), (krb,))
                    dve(lambda e, _b=bb, _m=tm1, _m2=tm2, _r=rbf: e.tensor_tensor(
                        _r[_b + 32:_b + 64, :], _m2[_b:_b + 32, :],
                        _m[_b + 32:_b + 64, :],
                        ALU.add), (ktm, ktm2), (krb,))
                dst_p = q_hbm if oc < 8 else k_hbm
                dst_r = qr_hbm if oc < 8 else kr_hbm
                ro = (oc % 8) * 128
                p.dma("pool", dst_p[ro:ro + 128, tg * 512:(tg + 1) * 512], obf[:],
                      reads=(kob,), writes=("qk_hbm",))
                p.dma("pool", dst_r[ro:ro + 128, tg * 512:(tg + 1) * 512], rbf[:],
                      reads=(krb,), writes=("qk_hbm",))
            else:
                act(lambda e, _b=bank, _o=obf: e.copy(_o[:], _b[:]), (bk,), (kob, bk))
                ro = (oc - 16) * 128
                p.dma("pool", v_hbm[ro:ro + 128, tg * 512:(tg + 1) * 512], obf[:],
                      reads=(kob,), writes=("v_hbm",))
    A32.pop(); A16.pop()
    p.barrier()
    if upto < 8:
        p.finalize_and_emit()
        return nc

    p.tagctx = "l1w1"
    A32.push(); A16.push()
    w1h = A16.take(8 * 2048)
    xsl8 = [A16.take(8 * 512), A16.take(8 * 512)]
    abf = A16.take(4096)
    ev1 = A16.take(1024)
    od1 = A16.take(32)
    od2 = A16.take(32)
    hst = A16.take(512)
    b1cs = A32.take(32)
    ev1f = A32.take(1024)
    cst2 = A32.take(512)
    ev2f = A32.take(1024)
    z16 = A32.take(1024)
    for fc in range(8):
        p.dma("act", od2[:, fc * 4:fc * 4 + 4],
              xnt_hbm[fc * 128:(fc + 1) * 128, 10:17:2], writes=("od2",),
              allow_slow_non_contiguous=True)
    p.dma("sp", b1cs[:], hb["b1c"], writes=("b1c",))
    for i in range(4):
        p.dma("sp", ev1f[32 * i:32 * i + 1, :], xrow_hbm[1 + 2 * i:2 + 2 * i, :],
              writes=("ev1f",))
        dve(lambda e, _i=i: e.tensor_copy(ev1[32 * _i:32 * _i + 1, :],
                                          ev1f[32 * _i:32 * _i + 1, :]),
            ("ev1f",), ("ev1",))
    for fc in range(8):
        p.dma("act", od1[:, fc * 4:fc * 4 + 4],
              xnt_hbm[fc * 128:(fc + 1) * 128, 2:9:2], writes=("od1",),
              allow_slow_non_contiguous=True)
    p.tagctx = "e2st"
    dve(lambda e: e.memset(z16[:], 0.0), (), ("z16",))
    p.dma("act", e2_hbm, z16[0:16, :], reads=("z16",), writes=("e2_hbm",))
    for i in range(4):
        p.dma("act", ev2f[32 * i:32 * i + 1, :], xrow_hbm[9 + 2 * i:10 + 2 * i, :],
              writes=("ev2f",))
    for qq in range(4):
        for i in range(4):
            p.dma("act", e2_hbm[qq * 4 + i:qq * 4 + i + 1, i * 256:(i + 1) * 256],
                  ev2f[32 * i:32 * i + 1, qq:qq + 1021:4], reads=("ev2f", "e2_hbm"),
                  writes=("e2_hbm",), allow_slow_non_contiguous=True)
    p.tagctx = "l1w1"
    load_xnt(xsl8[0], 0, "wx0")
    for tg in range(8):
        p.tagctx = f"l1a{tg}"
        bi = tg % 2
        kxs = f"wx{bi}"
        if tg + 1 < 8:
            load_xnt(xsl8[1 - bi], tg + 1, f"wx{1 - bi}")
        for i in range(4):
            bank, bk = psb[4 + i % 2], PS[4 + i % 2]
            for fc in range(8):
                mm(bank[0:1, :], od1[:, fc * 4 + i:fc * 4 + i + 1],
                   xsl8[bi][:, fc * 512:(fc + 1) * 512], ("od1", kxs), bk,
                   start=(fc == 0), stop=(fc == 7), tag="apass")
            def aev(e, _i=i, _tg=tg, _b=bank):
                return e.tensor_scalar(abf[32 * _i:32 * _i + 1, _tg * 512:(_tg + 1) * 512],
                                       _b[0:1, :], rowc[0:1, 3:4], None, ALU.mult)
            dve(aev, (bk, CONST), ("abf", bk))
    for half in range(2):
        p.tagctx = f"l1w1h{half}"
        for fc in range(8):
            p.dma("pool", w1h[:, fc * 2048:(fc + 1) * 2048],
                  hb["W1T"][:, fc * 4096 + half * 2048:fc * 4096 + (half + 1) * 2048],
                  writes=("w1h",), tag="w1ld")
        load_xnt(xsl8[0], 0, "wx0")
        for tg in range(8):
            p.tagctx = f"l1w1h{half}t{tg}"
            bi = tg % 2
            kxs = f"wx{bi}"
            if tg + 1 < 8:
                load_xnt(xsl8[1 - bi], tg + 1, f"wx{1 - bi}")
            for oci in range(16):
                oc = half * 16 + oci
                bank, bk = psb[oci % 4], PS[oci % 4]
                for fc in range(8):
                    mm(bank[:], w1h[:, fc * 2048 + oci * 128:fc * 2048 + (oci + 1) * 128],
                       xsl8[bi][:, fc * 512:(fc + 1) * 512], ("w1h", kxs), bk,
                       start=(fc == 0), stop=False)
                i_l = oc // 8
                mm(bank[:], ev1[32 * i_l:32 * i_l + 1, (oc % 8) * 128:(oc % 8 + 1) * 128],
                   abf[32 * i_l:32 * i_l + 1, tg * 512:(tg + 1) * 512],
                   ("ev1", "abf"), bk, start=False, stop=True, tag="lora1",
                   tp=((96, 0) if i_l == 3 else None))
                act(lambda e, _o=oc, _b=bank: e.activation(
                    hst[:], _b[:], AF.Gelu, bias=b1cs[:, _o:_o + 1]),
                    (bk, "b1c"), ("hst", bk))
                p.dma("pool", h_hbm[oc * 128:(oc + 1) * 128, tg * 512:(tg + 1) * 512],
                      hst[:], reads=("hst",), writes=("h_hbm",))
                jq = oc // 8
                cbank, cpk = psb[4 + (jq % 2)], PS[4 + (jq % 2)]
                mm(cbank[0:4, :], od2[:, (oc % 8) * 4:(oc % 8) * 4 + 4], hst[:],
                   ("od2", "hst"), cpk, start=(oc % 8 == 0), stop=(oc % 8 == 7),
                   tag="cacc2")
                if oc % 8 == 7:
                    dve(lambda e, _cb=cbank: e.tensor_scalar(
                        cst2[0:4, :], _cb[0:4, :], misc[0:4, 2:3], None, ALU.mult),
                        (cpk,), ("cst2", cpk), sreads=(CONST,))
                    p.dma("pool", c_hbm[0:4, jq * 4096 + tg * 512:
                                        jq * 4096 + (tg + 1) * 512],
                          cst2[0:4, :], reads=("cst2",), writes=("c_hbm",))
    A32.pop(); A16.pop()
    p.barrier()
    if upto < 9:
        p.finalize_and_emit()
        return nc

    p.tagctx = "attn"
    A32.push(); A16.push()
    amk = A32.take(2048)
    denc = A32.take(32)
    rdts = A32.take(128)
    rdtf = A32.take(128)
    rdb = A32.take(512)
    for mi in range(4):
        p.dma("sp", amk[:, mi * 512:(mi + 1) * 512], hb[f"amask{mi}"], writes=("amk",))
    A16.push()
    qsl = [A16.take(4096), A16.take(4096)]
    ksl = [A16.take(4096), A16.take(4096)]
    dumeb = [A16.take(4096), A16.take(4096), A16.take(4096)]
    rdt1 = A16.take(128)
    rdt2 = A16.take(128)

    def dload(h, bi):
        r0 = h * 64
        p.dma("sp", qsl[bi][0:64, :], q_hbm[r0:r0 + 64, :], writes=(f"qs{bi}",))
        p.dma("sp", ksl[bi][0:64, :], k_hbm[r0:r0 + 64, :], writes=(f"ks{bi}",))

    dload(0, 0)
    for hh in range(AH):
        p.tagctx = f"atd{hh}"
        bi = hh % 2
        if hh + 1 < AH:
            dload(hh + 1, 1 - bi)
        kq, kk = f"qs{bi}", f"ks{bi}"
        for it in range(32):
            du = dumeb[it % 3]
            kdu = f"du{it % 3}"
            for jp in range(4):
                pi = (4 * it + jp) % 3
                k0, k1 = PS[2 * pi], PS[2 * pi + 1]
                mm(psb[2 * pi][:], qsl[bi][0:64, it * 128:(it + 1) * 128],
                   ksl[bi][0:64, (2 * jp) * 512:(2 * jp + 1) * 512], (kq, kk), k0,
                   tag="den")
                mm(psb[2 * pi + 1][:], qsl[bi][0:64, it * 128:(it + 1) * 128],
                   ksl[bi][0:64, (2 * jp + 1) * 512:(2 * jp + 2) * 512], (kq, kk), k1,
                   tag="den")
                act(lambda e, _pi=pi, _jp=jp, _du=du: e.activation(
                    _du[:, _jp * 1024:(_jp + 1) * 1024], pspair[_pi][:], AF.Exp,
                    scale=0.125), (k0, k1, kdu), (kdu, k0, k1))
            dve(lambda e, _it=it, _du=du: e.reduce_sum(
                denc[:, _it:_it + 1], _du[:], axis=AX.X), (kdu,), ("denc", kdu))
        dve(lambda e: e.reciprocal(denc[:], denc[:]), ("denc",), ("denc",))
        tr(psb[5][0:32, 0:128], denc[:], ident[:], ("denc", CONST), PS[5], tag="dtr")
        dve(lambda e: e.tensor_copy(rdts[0:32, :], psb[5][0:32, 0:128]),
            (PS[5],), ("rdt", PS[5]))
        dve(lambda e: e.tensor_copy(rdt1[0:32, :], rdts[0:32, :]), ("rdt",), ("rd1",))
        dve(lambda e: e.tensor_copy(rdtf[0:32, :], rdt1[0:32, :]), ("rd1",), ("rdf",))
        dve(lambda e: e.tensor_tensor(rdtf[0:32, :], rdts[0:32, :], rdtf[0:32, :],
                                      ALU.subtract), ("rdt", "rdf"), ("rdf",))
        dve(lambda e: e.tensor_copy(rdt2[0:32, :], rdtf[0:32, :]), ("rdf",), ("rd2",))
        p.dma("pool", rden_hbm[2 * hh:2 * hh + 1, :].rearrange("r (a b) -> (r a) b", a=32),
              rdt1[0:32, :], reads=("rd1",), writes=("rden_hbm",))
        p.dma("pool", rden_hbm[2 * hh + 1:2 * hh + 2, :].rearrange("r (a b) -> (r a) b", a=32),
              rdt2[0:32, :], reads=("rd2",), writes=("rden_hbm",))
    A16.pop()
    p.barrier()
    A16.push()
    qrl = [A16.take(4096), A16.take(4096)]
    krl = [A16.take(4096), A16.take(4096)]
    vfl = [A16.take(4096), A16.take(4096)]
    vtok = A16.take(2048)
    attnT = A16.take(4096)
    rdf16 = [A16.take(4096), A16.take(4096)]
    ybst = A16.take(512)

    def nload(h, bi):
        r0 = h * 64
        p.dma("sp", qrl[bi][0:64, :], qr_hbm[r0:r0 + 64, :], writes=(f"qr{bi}",))
        p.dma("sp", krl[bi][0:64, :], kr_hbm[r0:r0 + 64, :], writes=(f"kr{bi}",))
        p.dma("sp", vfl[bi][0:64, :], v_hbm[r0:r0 + 64, :], writes=(f"vf{bi}",))
        p.dma("sp", rdf16[bi][0:2, :], rden_hbm[2 * h:2 * h + 2, :],
              writes=(f"rdf{bi}",))

    nload(0, 0)
    for hh in range(AH):
        p.tagctx = f"at{hh}"
        bi = hh % 2
        r0 = hh * 64
        if hh + 1 < AH:
            nload(hh + 1, 1 - bi)
        kq, kk, kv, krd = f"qr{bi}", f"kr{bi}", f"vf{bi}", f"rdf{bi}"
        for jt in range(32):
            pk, pb_ = PSB[jt % 2], psbb[jt % 2]
            tr(pb_[:, 0:64], vfl[bi][0:64, jt * 128:(jt + 1) * 128],
               identb[0:64, 0:64], (kv, CONST), pk, tag="vtr")
            dve(lambda e, _j=jt, _pb=pb_: e.tensor_copy(
                vtok[:, _j * 64:(_j + 1) * 64], _pb[:, 0:64]), (pk,), (f"vtok{jt}", pk))
        for ig in range(8):
            njt = 4 * ig + 4

            def avmm(jt, _njt=njt):
                mm(psb[4][0:64, :], vtok[:, jt * 64:(jt + 1) * 64],
                   attnT[:, (jt % 8) * 512:(jt % 8 + 1) * 512],
                   (f"vtok{jt}", f"aT{jt % 8}"), PS[4],
                   start=(jt == 0), stop=(jt == _njt - 1), tag="av")

            for jt2 in range(njt // 2):
                pi = jt2 % 2
                k0, k1 = PS[2 * pi], PS[2 * pi + 1]
                jta, jtb = 2 * jt2, 2 * jt2 + 1
                sa = jta % 8
                mm(psb[2 * pi][:], krl[bi][0:64, jta * 128:(jta + 1) * 128],
                   qrl[bi][0:64, ig * 512:(ig + 1) * 512], (kk, kq), k0, tag="numm")
                mm(psb[2 * pi + 1][:], krl[bi][0:64, jtb * 128:(jtb + 1) * 128],
                   qrl[bi][0:64, ig * 512:(ig + 1) * 512], (kk, kq), k1, tag="numm")
                act(lambda e, _pi=pi, _s=sa: e.activation(
                    attnT[:, _s * 512:(_s + 2) * 512], pspair[_pi][:], AF.Exp,
                    scale=0.125), (k0, k1, f"aT{sa}", f"aT{sa + 1}"),
                    (f"aT{sa}", f"aT{sa + 1}", k0, k1))
                for jt, mi in ((jta, jta - 4 * ig), (jtb, jtb - 4 * ig)):
                    if mi >= 0:
                        dve(lambda e, _s=jt % 8, _m=mi: e.tensor_tensor(
                            attnT[:, _s * 512:(_s + 1) * 512],
                            attnT[:, _s * 512:(_s + 1) * 512],
                            amk[:, _m * 512:(_m + 1) * 512], ALU.mult),
                            (f"aT{jt % 8}", "amk"), (f"aT{jt % 8}",))
                if jt2 >= 2:
                    avmm(2 * jt2 - 4)
                    avmm(2 * jt2 - 3)
            for jt in range(max(0, njt - 4), njt):
                avmm(jt)
            mm(psb[5][:], ones2b[:, 0:128], rdf16[bi][0:2, ig * 512:(ig + 1) * 512],
               (CONST, krd), PS[5], tag="rdb")
            dve(lambda e: e.tensor_copy(rdb[0:64, :], psb[5][0:64, :]), (PS[5],),
                ("rdb", PS[5]))
            dve(lambda e: e.tensor_tensor(ybst[0:64, :], psb[4][0:64, :], rdb[0:64, :],
                                          ALU.mult), (PS[4], "rdb"), ("ybst", PS[4]))
            p.dma("pool", y_hbm[r0:r0 + 64, ig * 512:(ig + 1) * 512], ybst[0:64, :],
                  reads=("ybst",), writes=("y_hbm",))
    A16.pop()
    A32.pop(); A16.pop()
    p.barrier()
    if upto < 11:
        p.finalize_and_emit()
        return nc

    A32.push(); A16.push()
    c16 = A16.take(4096)
    e2b = A16.take(1024)
    for qq in range(4):
        p.dma("pool", c16[qq * 4:(qq + 1) * 4, :],
              c_hbm[:, qq * T:(qq + 1) * T], writes=("c16",))
    p.dma("pool", e2b[0:16, :], e2_hbm, writes=("c16",))

    p.tagctx = "s1"
    A32.push(); A16.push()
    wo = A16.take(8 * 1024)
    p.dma("pool", wo[:], hb["Wout"], writes=("wo",))
    boutb = A16.take(1024)
    boutf = A32.take(1024)
    p.dma("sp", boutf[0:1, :], hb["bout"], writes=("boutf",))
    dve(lambda e: e.tensor_copy(boutb[0:1, :], boutf[0:1, :]), ("boutf",), ("boutb",))
    ysl = [A16.take(4096), A16.take(4096)]
    s1st = A32.take(1024)

    def yld(t4, bi):
        for fc in range(8):
            p.dma("sp", ysl[bi][:, fc * 512:(fc + 1) * 512],
                  y_hbm[fc * 128:(fc + 1) * 128, t4 * 512:(t4 + 1) * 512],
                  writes=(f"ysl{bi}",))

    yld(0, 0)
    for t4 in range(8):
        bi = t4 % 2
        if t4 + 1 < 8:
            yld(t4 + 1, 1 - bi)
        for ti in range(4):
            tt = t4 * 4 + ti
            p.tagctx = f"s1_{tt}"
            for og in range(2):
                bank, bk = psb[og], PS[og]
                for fc in range(8):
                    mm(bank[:], ysl[bi][:, fc * 512 + ti * 128:fc * 512 + (ti + 1) * 128],
                       wo[:, fc * 1024 + og * 512:fc * 1024 + (og + 1) * 512],
                       (f"ysl{bi}", "wo"), bk, start=(fc == 0), stop=False)
                mm(bank[:], onesb[:], boutb[0:1, og * 512:(og + 1) * 512],
                   (CONST, "boutb"), bk, start=False, stop=True, tag="bout")
                dve(lambda e, _b=bank, _og=og: e.tensor_copy(
                    s1st[:, _og * 512:(_og + 1) * 512], _b[:]), (bk,), ("s1st", bk))
            p.dma("pool", s1_hbm[tt * 128:(tt + 1) * 128, :], s1st[:],
                  reads=("s1st",), writes=("s1_hbm",))
    A32.pop(); A16.pop()
    p.barrier()
    if upto < 12:
        p.finalize_and_emit()
        return nc

    p.tagctx = "w2"
    A32.push(); A16.push()
    hbig = [A16.take(16384), A16.take(16384)]
    w2sl = [A16.take(2048), A16.take(2048)]
    b2cs = A32.take(8)
    g1f = A32.take(8 * 512)
    g1tm = A32.take(1024)
    xs1 = A32.take(2048)
    x2st = A32.take(1024)
    p.dma("sp", b2cs[:], hb["b2c"], writes=("b2c",))

    def hload(tg, hbi):
        t0 = tg * 512
        for h4 in range(8):
            p.dma("sp", hbig[hbi][:, h4 * 2048:(h4 + 1) * 2048]
                  .rearrange("p (c t) -> p c t", c=4),
                  h_hbm[h4 * 4 * 128:(h4 * 4 + 4) * 128, t0:t0 + 512]
                  .rearrange("(c p) t -> p c t", p=128), writes=(f"hb{hbi}",))

    hload(0, 0)
    for tg in range(8):
        p.tagctx = f"w2t{tg}"
        t0 = tg * 512
        hbi = tg % 2
        kh = f"hb{hbi}"
        if tg + 1 < 8:
            hload(tg + 1, 1 - hbi)
        for op2 in range(2):
            for h4 in range(8):
                bi = h4 % 2
                kw = f"w2{bi}"
                hh0 = h4 * 4
                p.dma("pool", w2sl[bi][:].rearrange("p (c t) -> p c t", c=4),
                      hb["W2T"][:, hh0 * 1024:(hh0 + 4) * 1024]
                      .rearrange("p (c t) -> p c t", t=1024)[:, :, op2 * 512:(op2 + 1) * 512],
                      writes=(kw,))
                for ci in range(4):
                    for oc4 in range(4):
                        mm(psb[oc4][:],
                           w2sl[bi][:, ci * 512 + oc4 * 128:ci * 512 + (oc4 + 1) * 128],
                           hbig[hbi][:, h4 * 2048 + ci * 512:h4 * 2048 + (ci + 1) * 512],
                           (kh, kw), PS[oc4], start=(h4 == 0 and ci == 0), stop=False)
            for oc4 in range(4):
                oc = op2 * 4 + oc4
                mm(psb[oc4][:], e2b[0:16, oc * 128:(oc + 1) * 128],
                   c16[0:16, t0:t0 + 512], ("c16",), PS[oc4],
                   start=False, stop=True, tag="lora2")
                act(lambda e, _o=oc, _b=psb[oc4]: e.activation(
                    g1f[:, _o * 512:(_o + 1) * 512], _b[:], AF.Silu,
                    bias=b2cs[:, _o:_o + 1]), (PS[oc4], "b2c"), ("g1f", PS[oc4]))
        for ti in range(4):
            tt = tg * 4 + ti
            for oc in range(8):
                pk, pb_ = PS[4 + (oc % 2)], psb[4 + (oc % 2)]
                tr(pb_[:, 0:128], g1f[:, oc * 512 + ti * 128:oc * 512 + (ti + 1) * 128],
                   ident[:], ("g1f", CONST), pk)
                dve(lambda e, _o=oc, _pb=pb_: e.tensor_copy(
                    g1tm[:, _o * 128:(_o + 1) * 128], _pb[:, 0:128]), (pk,), ("g1tm", pk))
            p.dma("sp", xs1[:, 0:1024], x1_hbm[tt * 128:(tt + 1) * 128, :], writes=("xs1",))
            p.dma("sp", xs1[:, 1024:2048], s1_hbm[tt * 128:(tt + 1) * 128, :], writes=("xs1",))
            dve(lambda e: e.tensor_tensor(x2st[:], xs1[:, 1024:2048], g1tm[:], ALU.mult),
                ("xs1", "g1tm"), ("x2st",))
            dve(lambda e: e.tensor_tensor(x2st[:], x2st[:], xs1[:, 0:1024], ALU.add),
                ("x2st", "xs1"), ("x2st",))
            p.dma("pool", out_t[tt * 128:(tt + 1) * 128, :], x2st[:],
                  reads=("x2st",), writes=("out",))
    A32.pop(); A16.pop()
    A32.pop(); A16.pop()
    p.barrier()
    p.finalize_and_emit()
    return nc


_BUILD_CACHE = {}


def _get_program(upto=99, dbg=()):
    key = (upto, tuple(sorted(dbg)))
    if key not in _BUILD_CACHE:
        _BUILD_CACHE[key] = build_program(upto=upto, dbg=dbg)
    return _BUILD_CACHE[key]


def kernel(**inputs):
    consts = _host_constants()
    w = _prep_weights(inputs)
    nc = _get_program()
    x_full = np.asarray(inputs["L0_attn_u"], dtype=np.float32)
    base = {}
    for k, v in consts.items():
        base[k] = v
    for k, v in w.items():
        if k in _W_BF16:
            v = np.asarray(v).astype(bf16)
        base[k] = np.ascontiguousarray(v)
    in_maps = []
    for core in range(8):
        m = dict(base)
        m["x"] = np.ascontiguousarray(x_full[core])
        in_maps.append(m)
    res = run_bass_kernel_spmd(nc, in_maps, core_ids=list(range(8)))
    out = np.stack([np.asarray(res.results[i]["out"]) for i in range(8)], axis=0)
    return out.astype(np.float32)


# version 23
# speedup vs baseline: 1.0059x; 1.0059x over previous
import numpy as np
import ml_dtypes
import concourse.bass as bass
import concourse.mybir as mybir
from concourse.bass_utils import run_bass_kernel_spmd

F32 = mybir.dt.float32
BF16 = mybir.dt.bfloat16
AF = mybir.ActivationFunctionType
ALU = mybir.AluOpType
AX = mybir.AxisListType
bf16 = ml_dtypes.bfloat16

T = 4096
D = 1024
NCH, CW = 32, 128
AH = 16


class _Op:
    __slots__ = ("eng", "emit", "deps", "inc", "idx", "milestone", "tag",
                 "dma_sem", "dma_val")
    def __init__(self, eng, emit, idx, tag):
        self.eng, self.emit, self.idx, self.tag = eng, emit, idx, tag
        self.deps = set()
        self.inc = False
        self.milestone = None
        self.dma_sem = None
        self.dma_val = None


class Prog:
    DMA_ENGS = ("sp", "pool", "act", "dve")

    def __init__(self, nc):
        self.nc = nc
        self.engs = ("pe", "dve", "act", "sp", "pool")
        self.sems = {e: nc.alloc_semaphore(name=f"sem_{e}") for e in self.engs}
        self.ops = {e: [] for e in self.engs}
        self.last_writer = {}
        self.readers = {}
        self.tagctx = ""
        self.dma_sem_of = {}
        self.dma_counts = {}
        self.dma_last = {}

    def op(self, eng, emit, reads=(), writes=(), sreads=(), tag=""):
        o = _Op(eng, emit, len(self.ops[eng]), tag or self.tagctx)
        deps = set()
        force = set()
        for k in reads:
            w = self.last_writer.get(k)
            if w is not None:
                deps.add(w)
            self.readers.setdefault(k, []).append(o)
        for k in sreads:
            w = self.last_writer.get(k)
            if w is not None:
                deps.add(w)
                force.add(w)
            self.readers.setdefault(k, []).append(o)
        for k in writes:
            w = self.last_writer.get(k)
            if w is not None:
                deps.add(w)
            for rd in self.readers.get(k, ()):
                deps.add(rd)
            self.last_writer[k] = o
            self.readers[k] = []
        for d in deps:
            if d is o:
                continue
            if d.eng == eng == "pe" and d not in force:
                continue
            o.deps.add(d)
            d.inc = True
        self.ops[eng].append(o)
        return o

    def dma(self, eng, out, in_, reads=(), writes=(), tag="", group=None, **kw):
        assert eng in Prog.DMA_ENGS
        def emit(e, _o=out, _i=in_, _k=kw):
            return e.dma_start(out=_o, in_=_i, **_k)
        gk = (eng, group if group is not None else (writes[0] if writes else "dma_misc"))
        if gk not in self.dma_sem_of:
            name = f"dsem{len(self.dma_sem_of)}"
            self.dma_sem_of[gk] = (name, self.nc.alloc_semaphore(name=name))
            self.dma_counts[name] = 0
        sname, shandle = self.dma_sem_of[gk]
        o = self.op(eng, emit, reads=reads, writes=writes, tag=tag or self.tagctx)
        prev = self.dma_last.get(sname)
        if prev is not None:
            o.deps.add(prev)
            prev.inc = True
        o.inc = True
        o.dma_sem = (sname, shandle)
        self.dma_counts[sname] += 16
        o.dma_val = self.dma_counts[sname]
        self.dma_last[sname] = o
        return o

    def barrier(self):
        lasts = [o for o in (self.ops[e][-1] if self.ops[e] else None
                             for e in self.engs) if o is not None]
        lasts += [o for o in self.dma_last.values() if o is not None]
        for e in self.engs:
            if not lasts:
                continue
            o = self.op(e, None, tag="barrier")
            for d in lasts:
                if d is o:
                    continue
                o.deps.add(d)
                d.inc = True

    def finalize_and_emit(self):
        for e, ops in self.ops.items():
            cnt = 0
            for o in ops:
                if o.inc:
                    if o.dma_sem is not None:
                        o.milestone = (o.dma_sem[0], o.dma_val)
                    else:
                        cnt += 1
                        o.milestone = (e, cnt)
        all_waits = {}
        for e, ops in self.ops.items():
            ws = []
            for o in ops:
                w = {}
                for d in o.deps:
                    sn, v = d.milestone
                    w[sn] = max(w.get(sn, 0), v)
                ws.append(w)
            all_waits[e] = ws
        for e in self.engs:
            hi = {}
            for w in all_waits[e]:
                for se in list(w):
                    if hi.get(se, 0) >= w[se]:
                        del w[se]
                    else:
                        hi[se] = w[se]
        self._simulate(all_waits)
        finals = dict(self.dma_counts)
        for e, ops in self.ops.items():
            tot = sum(1 for o in ops if o.inc and o.dma_sem is None)
            if tot:
                finals[e] = tot
        prog = self
        semmap = dict(self.sems)
        for sn, sh in self.dma_sem_of.values():
            semmap[sn] = sh

        def mk_body(ename):
            def body(eng_obj):
                for o, w in zip(prog.ops[ename], all_waits[ename]):
                    for se, v in w.items():
                        eng_obj.wait_ge(semmap[se], v)
                    if o.emit is not None:
                        inst = o.emit(eng_obj)
                        if o.inc:
                            if o.dma_sem is not None:
                                inst.then_inc(o.dma_sem[1], 16)
                            else:
                                inst.then_inc(prog.sems[ename], 1)
                    elif o.inc:
                        eng_obj.nop().then_inc(prog.sems[ename], 1)
                if ename == "sp":
                    for se, v in finals.items():
                        eng_obj.wait_ge(semmap[se], v)
            return body

        with self.nc.Block() as blk:
            blk.tensor(mk_body("pe"))
            blk.vector(mk_body("dve"))
            blk.scalar(mk_body("act"))
            blk.sync(mk_body("sp"))
            blk.gpsimd(mk_body("pool"))

    def _simulate(self, all_waits):
        pc = {e: 0 for e in self.engs}
        val = {}
        n = {e: len(self.ops[e]) for e in self.engs}
        progress = True
        while progress:
            progress = False
            for e in self.engs:
                while pc[e] < n[e]:
                    w = all_waits[e][pc[e]]
                    if any(val.get(se, 0) < v for se, v in w.items()):
                        break
                    o = self.ops[e][pc[e]]
                    if o.inc:
                        sn, v = o.milestone
                        val[sn] = v
                    pc[e] += 1
                    progress = True
        stuck = {e: pc[e] for e in self.engs if pc[e] < n[e]}
        if stuck:
            lines = [f"  {e}@{pcv}/{n[e]} tag={self.ops[e][pcv].tag} "
                     f"waits={all_waits[e][pcv]}"
                     for e, pcv in stuck.items()]
            raise RuntimeError("DEADLOCK in program build:\n" + "\n".join(lines))



def _host_constants():
    c = {}
    c["ident_f32"] = np.eye(128, dtype=np.float32)
    c["ident_bf"] = np.eye(128, dtype=np.float32).astype(bf16)
    c["maskT"] = np.triu(np.ones((128, 128), dtype=np.float32))
    cols = np.arange(512)[None, :]
    rows = np.arange(128)[:, None]
    for pp in range(4):
        c[f"amask{pp}"] = (cols >= rows + 128 * pp).astype(np.float32)
    i = np.arange(32, dtype=np.float64)
    theta = 10000.0 ** (-2.0 * (i - 1.0) / 64)
    ang = np.arange(T, dtype=np.float64)[:, None] * theta[None, :]
    cos = np.cos(ang).astype(np.float32).T
    sin = np.sin(ang).astype(np.float32).T
    c["cos2"] = np.ascontiguousarray(np.concatenate([cos] * 4, axis=0))
    c["sin2"] = np.ascontiguousarray(np.concatenate([sin] * 4, axis=0))
    return c


def _chunked(wmat):
    K, O = wmat.shape
    assert K % 128 == 0
    return np.ascontiguousarray(
        wmat.reshape(K // 128, 128, O).transpose(1, 0, 2).reshape(128, (K // 128) * O))


def _prep_weights(inp):
    w = {}
    w["WinT"] = _chunked(np.ascontiguousarray(np.asarray(inp["L0_attn_in_proj_w"]).T))
    w["convw"] = np.ascontiguousarray(
        np.asarray(inp["L0_attn_conv_w"]).reshape(18, 128, 4).transpose(1, 0, 2).reshape(128, 72))
    w["convb"] = np.ascontiguousarray(np.asarray(inp["L0_attn_conv_b"]).reshape(18, 128).T)
    w["dtbias"] = np.asarray(inp["L0_attn_dt_bias"]).reshape(32, 1)
    w["Alog"] = np.asarray(inp["L0_attn_A_log"]).reshape(32, 1)
    w["Dcols"] = np.ascontiguousarray(
        np.asarray(inp["L0_attn_D"]).reshape(16, 2).repeat(64, axis=1).reshape(16, 128).T)
    w["normw"] = np.ascontiguousarray(np.asarray(inp["L0_attn_norm_w"]).reshape(16, 128).T)
    w["OPT"] = _chunked(np.ascontiguousarray(np.asarray(inp["L0_attn_out_proj_w"]).T))
    w["cnw1"] = _chunked(np.asarray(inp["L0_ffn_cn_w1"]))
    w["cnw2"] = _chunked(np.asarray(inp["L0_ffn_cn_w2"]))
    w["cnw3"] = _chunked(np.asarray(inp["L0_ffn_cn_w3"]))
    w["cnw4"] = _chunked(np.asarray(inp["L0_ffn_cn_w4"]))
    w["gcw1"] = _chunked(np.asarray(inp["L0_ffn_gc_w1"]))
    w["gcw2"] = _chunked(np.asarray(inp["L0_ffn_gc_w2"]))
    w["gcw3"] = _chunked(np.asarray(inp["L0_ffn_gc_w3"]))
    w["cnb1"] = np.ascontiguousarray(np.asarray(inp["L0_ffn_cn_b1"]).reshape(8, 128).T)
    w["cnb2"] = np.ascontiguousarray(np.asarray(inp["L0_ffn_cn_b2"]).reshape(4, 128).T)
    w["cnb3"] = np.ascontiguousarray(np.asarray(inp["L0_ffn_cn_b3"]).reshape(2, 128).T)
    w["gcb1"] = np.ascontiguousarray(np.asarray(inp["L0_ffn_gc_b1"]).reshape(4, 128).T)
    w["gcb2"] = np.ascontiguousarray(np.asarray(inp["L0_ffn_gc_b2"]).reshape(2, 128).T)
    w["rowc"] = np.asarray([[float(np.asarray(inp["L0_ffn_cn_b4"])[0]),
                             float(np.asarray(inp["L0_ffn_gc_b3"])[0]),
                             float(np.asarray(inp["L0_ffn_adaptive_range"])[0]),
                             float(np.asarray(inp["L1_ffn_scale"])),
                             0.0, 0.0, 0.0, 0.0]], dtype=np.float32)
    Wqkv = np.asarray(inp["L1_attn_Wqkv"])
    perm = np.arange(3 * D)
    half = np.concatenate([np.arange(0, 64, 2), np.arange(1, 64, 2)])
    for b in range(2 * AH):
        perm[b * 64:(b + 1) * 64] = b * 64 + half
    w["WqkvP"] = _chunked(np.ascontiguousarray(Wqkv[:, perm]))
    w["Wout"] = _chunked(np.asarray(inp["L1_attn_Wout"]))
    w["bout"] = np.asarray(inp["L1_attn_bout"]).reshape(1, 1024)
    w["W1T"] = _chunked(np.ascontiguousarray(np.asarray(inp["L1_ffn_W1"]).T))
    w["b1c"] = np.ascontiguousarray(np.asarray(inp["L1_ffn_b1"]).reshape(32, 128).T)
    w["W2T"] = _chunked(np.ascontiguousarray(np.asarray(inp["L1_ffn_W2"]).T))
    w["b2c"] = np.ascontiguousarray(np.asarray(inp["L1_ffn_b2"]).reshape(8, 128).T)
    return w


_W_SHAPES = {
    "WinT": (128, 8 * 4384), "convw": (128, 72), "convb": (128, 18),
    "dtbias": (32, 1), "Alog": (32, 1), "Dcols": (128, 16), "normw": (128, 16),
    "OPT": (128, 16 * 1024),
    "cnw1": (128, 8 * 1024), "cnw2": (128, 8 * 512), "cnw3": (128, 4 * 256),
    "cnw4": (128, 2), "gcw1": (128, 8 * 512), "gcw2": (128, 4 * 256),
    "gcw3": (128, 2),
    "cnb1": (128, 8), "cnb2": (128, 4), "cnb3": (128, 2),
    "gcb1": (128, 4), "gcb2": (128, 2), "rowc": (1, 8),
    "WqkvP": (128, 8 * 3072), "Wout": (128, 8 * 1024), "bout": (1, 1024),
    "W1T": (128, 8 * 4096), "b1c": (128, 32), "W2T": (128, 32 * 1024),
    "b2c": (128, 8),
}
_W_BF16 = {"WinT", "OPT", "cnw1", "cnw2", "cnw3", "cnw4", "gcw1", "gcw2", "gcw3",
           "WqkvP", "Wout", "W1T", "W2T"}
_C_SHAPES = {
    "ident_f32": (128, 128), "ident_bf": (128, 128), "maskT": (128, 128),
    "amask0": (128, 512), "amask1": (128, 512), "amask2": (128, 512),
    "amask3": (128, 512), "cos2": (128, T), "sin2": (128, T),
}



def build_program(upto=99, dbg=()):
    nc = bass.Bass("TRN2", target_bir_lowering=False)
    p = Prog(nc)
    CONST = "CONST"

    def din(name, shape, dt=F32):
        return nc.dram_tensor(name, list(shape), dt, kind="ExternalInput").ap()

    x_in = din("x", (T, D))
    hb = {k: din(k, s, BF16 if k == "ident_bf" else F32) for k, s in _C_SHAPES.items()}
    for k, s in _W_SHAPES.items():
        hb[k] = din(k, s, BF16 if k in _W_BF16 else F32)

    out_t = nc.dram_tensor("out", [T, D], F32, kind="ExternalOutput").ap()

    def scratch(name, shape, dt=F32):
        kind = "ExternalOutput" if name in dbg else "Internal"
        return nc.dram_tensor(name, list(shape), dt, kind=kind).ap()

    xnt_hbm = scratch("xnt_hbm", (1024, T), BF16)
    z_hbm = scratch("z_hbm", (2048, T), BF16)
    xssd_hbm = scratch("xssd_hbm", (2048, T), BF16)
    B_hbm = scratch("B_hbm", (128, T), BF16)
    C_hbm = scratch("C_hbm", (128, T), BF16)
    s123_hbm = scratch("s123_hbm", (96, T), BF16)
    m123_hbm = scratch("m123_hbm", (96, T), BF16)
    e12_hbm = scratch("e12_hbm", (64, T), BF16)
    g_hbm = scratch("g_hbm", (T, D), BF16)
    x1_hbm = scratch("x1_hbm", (T, D))
    q_hbm = scratch("q_hbm", (1024, T), BF16)
    k_hbm = scratch("k_hbm", (1024, T), BF16)
    qr_hbm = scratch("qr_hbm", (1024, T), BF16)
    kr_hbm = scratch("kr_hbm", (1024, T), BF16)
    v_hbm = scratch("v_hbm", (1024, T), BF16)
    rden_hbm = scratch("rden_hbm", (2 * AH, T), BF16)
    xrow_hbm = scratch("xrow_hbm", (128, D))
    a_hbm = scratch("a_hbm", (4, T))
    h_hbm = scratch("h_hbm", (4096, T), BF16)
    c_hbm = scratch("c_hbm", (4, 4 * T))
    e2_hbm = scratch("e2_hbm", (16, D))
    y_hbm = scratch("y_hbm", (1024, T), BF16)
    s1_hbm = scratch("s1_hbm", (T, D))
    ydbg_hbm = scratch("ydbg_hbm", (2048, T)) if "ydbg_hbm" in dbg else None
    i0dbg_hbm = scratch("i0dbg_hbm", (128, 4096)) if "i0dbg_hbm" in dbg else None
    gydbg_hbm = scratch("gydbg_hbm", (2048, T)) if "gydbg_hbm" in dbg else None

    sbal = lambda n, s, d=F32: nc.alloc_sbuf_tensor("sb_" + n, list(s), d).ap()
    ident = sbal("ident", (128, 128))
    identb = sbal("identb", (128, 128), BF16)
    maskT = sbal("maskT", (128, 128))
    misc = sbal("misc", (128, 8))
    dtb_col = sbal("dtb_col", (32, 1))
    nega_col = sbal("nega_col", (32, 1))
    dcols = sbal("dcols", (128, 16))
    normw = sbal("normw", (128, 16))
    convb = sbal("convb", (128, 18))
    convw = sbal("convw", (128, 72))
    onesrow = sbal("onesrow", (1, 128))
    onesb = sbal("onesb", (1, 128), BF16)
    onesbc = sbal("onesbc", (128, 1), BF16)
    ones2b = sbal("ones2b", (2, 128), BF16)
    rowc = sbal("rowc", (1, 8))
    rowm = sbal("rowm", (1, 8))
    AR32, AR16 = 22400, 46300
    ar32 = sbal("ar32", (128, AR32))
    ar16 = sbal("ar16", (128, AR16), BF16)
    pspair = [nc.alloc_psum_tensor(f"pp{i}", [128, 1024], F32).ap() for i in range(3)]
    psb = [pspair[i // 2][:, (i % 2) * 512:(i % 2 + 1) * 512] for i in range(6)]
    psbb = [nc.alloc_psum_tensor(f"pbb{i}", [128, 1024], BF16).ap() for i in range(2)]
    PS = [f"ps{i}" for i in range(6)]
    PSB = [f"psb{i}" for i in range(2)]

    class Arena:
        def __init__(self, ap, total, base=0):
            self.ap, self.total, self.off = ap, total, base
            self.marks = []
        def take(self, cols):
            assert self.off + cols <= self.total, (self.off, cols, self.total)
            v = self.ap[:, self.off:self.off + cols]
            self.off += cols
            return v
        def push(self):
            self.marks.append(self.off)
        def pop(self):
            self.off = self.marks.pop()

    dtde = ar32[:, 0:4096]
    hstate = ar32[:, 4096:6144]
    states_sb = ar32[:, 6144:8192]
    A32 = Arena(ar32, AR32, base=8192)
    A16 = Arena(ar16, AR16, base=0)

    def mm(ps, lhsT, rhs, reads, wkey, start=True, stop=True, tp=None, tag=""):
        def emit(e, _p=ps, _l=lhsT, _r=rhs, _s=start, _t=stop, _tp=tp):
            return e.matmul(_p, _l, _r, start=_s, stop=_t, tile_position=_tp,
                            skip_group_check=True)
        return p.op("pe", emit, reads=tuple(reads) + (wkey,), writes=(wkey,),
                    tag=tag or p.tagctx)

    def tr(ps, in_, idn, reads, wkey, tag=""):
        def emit(e, _p=ps, _i=in_, _n=idn):
            return e.transpose(_p, _i, _n)
        return p.op("pe", emit, reads=tuple(reads) + (wkey,), writes=(wkey,),
                    tag=tag or p.tagctx)

    def dve(emitfn, reads, writes, sreads=(), tag=""):
        return p.op("dve", emitfn, reads=reads, writes=writes, sreads=sreads,
                    tag=tag or p.tagctx)

    def act(emitfn, reads, writes, sreads=(), tag=""):
        return p.op("act", emitfn, reads=reads, writes=writes, sreads=sreads,
                    tag=tag or p.tagctx)

    eps_c = misc[:, 0:1]
    one_c = misc[:, 1:2]

    p.tagctx = "P0"
    for name, dst in (("ident_f32", ident), ("ident_bf", identb), ("maskT", maskT),
                      ("dtbias", dtb_col), ("Dcols", dcols), ("normw", normw),
                      ("convb", convb), ("convw", convw), ("rowc", rowc)):
        p.dma("sp", dst[:], hb[name], writes=(CONST,))
    dve(lambda e: e.memset(misc[:], 0.0), (), (CONST,))
    dve(lambda e: e.memset(misc[:, 0:1], 1e-5), (CONST,), (CONST,))
    dve(lambda e: e.memset(misc[:, 1:2], 1.0), (CONST,), (CONST,))
    dve(lambda e: e.memset(onesrow[:], 1.0), (), (CONST,))
    dve(lambda e: e.memset(onesb[:], 1.0), (), (CONST,))
    dve(lambda e: e.memset(onesbc[:], 1.0), (), (CONST,))
    dve(lambda e: e.memset(ones2b[:], 1.0), (), (CONST,))
    p.dma("sp", nega_col[:], hb["Alog"], writes=(CONST,))
    act(lambda e: e.activation(nega_col[:], nega_col[:], AF.Exp), (CONST,), (CONST,))
    mm(psb[0][:, 0:1], onesrow[:], rowc[0:1, 3:4], (CONST,), PS[0], tag="sclB")
    dve(lambda e: e.tensor_copy(misc[:, 2:3], psb[0][:, 0:1]), (PS[0], CONST),
        (CONST, PS[0]))
    p.barrier()

    def phase_LN(src_hbm, layer):
        A32.push(); A16.push()
        xt = [A32.take(1024), A32.take(1024)]
        cols = [A32.take(8), A32.take(8)]
        xsq = [A32.take(1024), A32.take(1024)]
        xnf = [A32.take(1024), A32.take(1024)]
        xnb = [A16.take(1024), A16.take(1024)]
        xstg = [A16.take(1024), A16.take(1024)]
        for i in range(32):
            bi = i % 2
            p.tagctx = f"L{layer}ln{i}"
            _xt, _xsq, _xnf, _xnb, _xstg = xt[bi], xsq[bi], xnf[bi], xnb[bi], xstg[bi]
            mean_c, var_c, tmp_c, sd_c, rstd_c = (cols[bi][:, j:j + 1] for j in range(5))
            KX, KC, KSQ, KF, KB = (f"lnxt{bi}", f"lnc{bi}", f"lnsq{bi}", f"lnf{bi}",
                                   f"lnb{bi}")
            KSs = [f"lnstg{bi}_{j}" for j in range(8)]
            p.dma("sp" if i % 2 == 0 else "act", _xt[:],
                  src_hbm[i * 128:(i + 1) * 128, :], writes=(KX,))
            dve(lambda e, _a=mean_c, _b=_xt: e.reduce_sum(_a, _b[:], axis=AX.X),
                (KX,), (KC,))
            act(lambda e, _a=_xsq, _b=_xt, _v=var_c: e.activation(
                _a[:], _b[:], AF.Square, accum_out=_v), (KX,), (KC, KSQ))
            dve(lambda e, _a=mean_c: e.tensor_scalar_mul(_a, _a, 1.0 / D), (KC,), (KC,))
            dve(lambda e, _a=var_c: e.tensor_scalar_mul(_a, _a, 1.0 / D), (KC,), (KC,))
            dve(lambda e, _a=sd_c, _b=mean_c: e.tensor_tensor(_a, _b, _b, ALU.mult),
                (KC,), (KC,))
            dve(lambda e, _a=tmp_c, _b=var_c, _c=sd_c: e.tensor_tensor(
                _a, _b, _c, ALU.subtract), (KC,), (KC,))
            act(lambda e, _a=sd_c, _b=tmp_c: e.activation(_a, _b, AF.Ln, bias=eps_c),
                (KC, CONST), (KC,))
            act(lambda e, _a=rstd_c, _b=sd_c: e.activation(_a, _b, AF.Exp, scale=-0.5),
                (KC,), (KC,))
            dve(lambda e, _a=_xnf, _b=_xt, _m=mean_c, _r=rstd_c: e.tensor_scalar(
                _a[:], _b[:], _m, _r, ALU.subtract, ALU.mult), (KX,), (KF,),
                sreads=(KC,))
            act(lambda e, _a=_xnb, _b=_xnf: e.copy(_a[:], _b[:]), (KF,), (KB,))
            if layer == 1 and i == 0:
                p.dma("pool", xrow_hbm, _xnf[:], reads=(KF,), writes=("xrow_hbm",))
            if layer == 0:
                for g2 in range(2):
                    mm(psb[4 + g2][0:1, :], onesbc[:], _xnb[:, g2 * 512:(g2 + 1) * 512],
                       (CONST, KB), PS[4 + g2], start=(i == 0), stop=(i == 31), tag="gcacc")
            for fc in range(8):
                pk = PSB[fc % 2]
                pb_ = psbb[fc % 2]
                tr(pb_[:, 0:128], _xnb[:, fc * 128:(fc + 1) * 128], identb[:],
                   (KB, CONST), pk)
                if fc % 2 == 0:
                    dve(lambda e, _f=fc, _pb=pb_, _s=_xstg: e.tensor_copy(
                        _s[:, _f * 128:(_f + 1) * 128], _pb[:, 0:128]),
                        (pk,), (KSs[fc], pk))
                else:
                    act(lambda e, _f=fc, _pb=pb_, _s=_xstg: e.copy(
                        _s[:, _f * 128:(_f + 1) * 128], _pb[:, 0:128]),
                        (pk,), (KSs[fc], pk))
            p.dma("pool", xnt_hbm[:, i * 128:(i + 1) * 128]
                  .rearrange("(c p) t -> p c t", p=128),
                  _xstg[:].rearrange("p (c t) -> p c t", c=8),
                  reads=tuple(KSs), writes=("xnt_hbm",))
        A32.pop(); A16.pop()

    p.tagctx = "wpre"
    A32.push(); A16.push()
    cw1 = A16.take(8 * 1024)
    cw2 = A16.take(8 * 512)
    cw3 = A16.take(4 * 256)
    cw4 = A16.take(2)
    gw1 = A16.take(8 * 512)
    gw2 = A16.take(4 * 256)
    gw3 = A16.take(2)
    cb1 = A32.take(8)
    cb2 = A32.take(4)
    cb3 = A32.take(2)
    gb1 = A32.take(4)
    gb2 = A32.take(2)
    p.dma("pool", cw1[:], hb["cnw1"], writes=("cw",))
    p.dma("act", cw2[:], hb["cnw2"], writes=("cw",))
    p.dma("act", cw3[:], hb["cnw3"], writes=("cw",))
    p.dma("act", cw4[:], hb["cnw4"], writes=("cw",))
    p.dma("act", gw1[:], hb["gcw1"], writes=("gw",))
    p.dma("act", gw2[:], hb["gcw2"], writes=("gw",))
    p.dma("act", gw3[:], hb["gcw3"], writes=("gw",))
    p.dma("act", cb1[:], hb["cnb1"], writes=("cw",))
    p.dma("act", cb2[:], hb["cnb2"], writes=("cw",))
    p.dma("act", cb3[:], hb["cnb3"], writes=("cw",))
    p.dma("act", gb1[:], hb["gcb1"], writes=("gw",))
    p.dma("act", gb2[:], hb["gcb2"], writes=("gw",))

    phase_LN(x_in, 0)
    p.barrier()

    p.tagctx = "gf"
    A32.push(); A16.push()
    gcb = A16.take(8)
    h1g = A16.take(4)
    h2g = A16.take(2)
    gcrow = A32.take(1024)
    gcf = A32.take(8)
    dve(lambda e: e.tensor_scalar_mul(gcrow[0:1, 0:512], psb[4][0:1, :], 1.0 / T),
        (PS[4],), ("gcrow", PS[4]))
    dve(lambda e: e.tensor_scalar_mul(gcrow[0:1, 512:1024], psb[5][0:1, :], 1.0 / T),
        (PS[5],), ("gcrow", PS[5]))
    for c8 in range(8):
        tr(psb[3][:, 0:1], gcrow[0:1, c8 * 128:(c8 + 1) * 128], ident[0:1, 0:1],
           ("gcrow", CONST), PS[3], tag="gctr")
        dve(lambda e, _c=c8: e.tensor_copy(gcf[:, _c:_c + 1], psb[3][:, 0:1]),
            (PS[3],), ("gcf", PS[3]))
    dve(lambda e: e.tensor_copy(gcb[:], gcf[:]), ("gcf",), ("gcb",))
    for j in range(4):
        for fc in range(8):
            mm(psb[0][:, j:j + 1], gw1[:, fc * 512 + j * 128:fc * 512 + (j + 1) * 128],
               gcb[:, fc:fc + 1], ("gw", "gcb"), PS[0],
               start=(j == 0 and fc == 0), stop=(fc == 7))
    for j in range(4):
        act(lambda e, _j=j: e.activation(h1g[:, _j:_j + 1], psb[0][:, _j:_j + 1],
                                         AF.Gelu, bias=gb1[:, _j:_j + 1]),
            (PS[0], "gw"), ("h1g",))
    for j in range(2):
        for fc in range(4):
            mm(psb[1][:, j:j + 1], gw2[:, fc * 256 + j * 128:fc * 256 + (j + 1) * 128],
               h1g[:, fc:fc + 1], ("gw", "h1g"), PS[1],
               start=(j == 0 and fc == 0), stop=(fc == 3))
    for j in range(2):
        act(lambda e, _j=j: e.activation(h2g[:, _j:_j + 1], psb[1][:, _j:_j + 1],
                                         AF.Gelu, bias=gb2[:, _j:_j + 1]),
            (PS[1], "gw"), ("h2g",))
    for fc in range(2):
        mm(psb[2][0:1, 0:1], gw3[:, fc:fc + 1], h2g[:, fc:fc + 1], ("gw", "h2g"), PS[2],
           start=(fc == 0), stop=(fc == 1))
    act(lambda e: e.activation(rowm[0:1, 0:1], psb[2][0:1, 0:1], AF.Sigmoid,
                               bias=rowc[0:1, 1:2]), (PS[2], CONST), ("rowm",))
    dve(lambda e: e.tensor_scalar_mul(rowm[0:1, 2:3], rowc[0:1, 2:3], 0.6),
        (CONST,), ("rowm",))
    dve(lambda e: e.tensor_scalar(rowm[0:1, 1:2], rowm[0:1, 0:1], rowc[0:1, 2:3], 1.4,
                                  ALU.mult, ALU.mult), (), ("rowm",),
        sreads=("rowm", CONST))
    dve(lambda e: e.tensor_scalar(rowm[0:1, 3:4], rowc[0:1, 2:3], -1.0, 0.1,
                                  ALU.mult, ALU.add), (), ("rowm",), sreads=(CONST,))
    dve(lambda e: e.tensor_tensor(rowm[0:1, 1:2], rowm[0:1, 1:2], rowm[0:1, 3:4],
                                  ALU.add), ("rowm",), ("rowm",))
    A32.pop(); A16.pop()
    p.barrier()
    if upto < 3:
        p.finalize_and_emit()
        return nc

    def load_xnt(buf, tg, key, eng="sp"):
        p.dma(eng, buf[:].rearrange("p (c t) -> p c t", c=8),
              xnt_hbm.rearrange("(c p) t -> p c t", p=128)
              [:, :, tg * 512:(tg + 1) * 512], writes=(key,))

    p.tagctx = "ffn0"
    A32.push(); A16.push()
    BT = 512
    outA = A32.take(8 * BT)
    outB = A32.take(8 * BT)
    xsl = [A16.take(8 * BT), A16.take(8 * BT)]
    rhsb = A16.take(8 * BT)
    h1 = A16.take(8 * BT)
    h2 = A16.take(4 * BT)
    h3 = A16.take(2 * BT)
    tfrow = A32.take(BT)
    gstg = A16.take(1024)
    permq = [4, 5, 6, 7, 0, 1, 2, 3]
    signq = [1.0, 1.0, -1.0, -1.0, -1.0, -1.0, 1.0, 1.0]
    load_xnt(xsl[0], 0, "fxs0")
    for blk_i in range(8):
        p.tagctx = f"ffn0b{blk_i}"
        bi = blk_i % 2
        kxs = f"fxs{bi}"
        if blk_i + 1 < 8:
            load_xnt(xsl[1 - bi], blk_i + 1, f"fxs{1 - bi}")
        KOA = [f"ffA{j}" for j in range(8)]
        KOB = [f"ffB{j}" for j in range(8)]
        KOpair = [KOA, KOB]
        for fc in range(8):
            act(lambda e, _f=fc, _x=xsl[bi]: e.copy(
                outA[:, _f * BT:(_f + 1) * BT], _x[:, _f * BT:(_f + 1) * BT]),
                (kxs,), (KOA[fc],))
        cur, nxt = 0, 1
        for cyc in range(3):
            curks, nxtks = KOpair[cur], KOpair[nxt]
            curv = outA if cur == 0 else outB
            nxtv = outA if nxt == 0 else outB
            if cyc == 0:
                rhs_view = [xsl[bi][:, fc * BT:(fc + 1) * BT] for fc in range(8)]
                rhs_keys = [kxs] * 8
            else:
                for fc in range(8):
                    act(lambda e, _f=fc, _cv=curv: e.copy(
                        rhsb[:, _f * BT:(_f + 1) * BT], _cv[:, _f * BT:(_f + 1) * BT]),
                        (curks[fc],), (f"rhsb{fc}",))
                rhs_view = [rhsb[:, fc * BT:(fc + 1) * BT] for fc in range(8)]
                rhs_keys = [f"rhsb{fc}" for fc in range(8)]
            for oc in range(8):
                bank, bk = psb[oc % 4], PS[oc % 4]
                for fc in range(8):
                    mm(bank[:], cw1[:, fc * 1024 + oc * 128:fc * 1024 + (oc + 1) * 128],
                       rhs_view[fc], ("cw", rhs_keys[fc]), bk,
                       start=(fc == 0), stop=(fc == 7))
                act(lambda e, _o=oc, _b=bank: e.activation(
                    h1[:, _o * BT:(_o + 1) * BT], _b[:], AF.Gelu, bias=cb1[:, _o:_o + 1]),
                    (bk, "cw"), (f"h1_{oc}",))
            for oc in range(4):
                bank, bk = psb[oc % 4], PS[oc % 4]
                for fc in range(8):
                    mm(bank[:], cw2[:, fc * 512 + oc * 128:fc * 512 + (oc + 1) * 128],
                       h1[:, fc * BT:(fc + 1) * BT], ("cw", f"h1_{fc}"), bk,
                       start=(fc == 0), stop=(fc == 7))
                act(lambda e, _o=oc, _b=bank: e.activation(
                    h2[:, _o * BT:(_o + 1) * BT], _b[:], AF.Gelu, bias=cb2[:, _o:_o + 1]),
                    (bk, "cw"), (f"h2_{oc}",))
            for oc in range(2):
                bank, bk = psb[oc % 2], PS[oc % 2]
                for fc in range(4):
                    mm(bank[:], cw3[:, fc * 256 + oc * 128:fc * 256 + (oc + 1) * 128],
                       h2[:, fc * BT:(fc + 1) * BT], ("cw", f"h2_{fc}"), bk,
                       start=(fc == 0), stop=(fc == 3))
                act(lambda e, _o=oc, _b=bank: e.activation(
                    h3[:, _o * BT:(_o + 1) * BT], _b[:], AF.Gelu, bias=cb3[:, _o:_o + 1]),
                    (bk, "cw"), (f"h3_{oc}",))
            for fc in range(2):
                mm(psb[2][0:1, :], cw4[:, fc:fc + 1], h3[:, fc * BT:(fc + 1) * BT],
                   ("cw", f"h3_{fc}"), PS[2], start=(fc == 0), stop=(fc == 1))
            act(lambda e: e.activation(tfrow[0:1, :], psb[2][0:1, :], AF.Sigmoid,
                                       bias=rowc[0:1, 0:1]), (PS[2], CONST), ("tf",))
            dve(lambda e: e.tensor_scalar(tfrow[0:1, :], tfrow[0:1, :],
                                          rowm[0:1, 2:3], rowm[0:1, 1:2],
                                          ALU.mult, ALU.add), ("tf",), ("tf",),
                sreads=("rowm",))
            mm(psb[3][:], onesrow[:], tfrow[0:1, :], (CONST, "tf"), PS[3], tag="coupB")
            for fc in range(8):
                def up(e, _f=fc, _s=permq[fc], _sg=signq[fc], _cv=curv, _nv=nxtv):
                    return e.scalar_tensor_tensor(
                        _nv[:, _f * BT:(_f + 1) * BT], _cv[:, _s * BT:(_s + 1) * BT],
                        _sg, psb[3][:], ALU.mult, ALU.mult)
                dve(up, (curks[permq[fc]], PS[3]), (nxtks[fc], PS[3]))
                def up2(e, _f=fc, _cv=curv, _nv=nxtv):
                    return e.tensor_tensor(
                        _nv[:, _f * BT:(_f + 1) * BT], _nv[:, _f * BT:(_f + 1) * BT],
                        _cv[:, _f * BT:(_f + 1) * BT], ALU.add)
                dve(up2, (curks[fc], nxtks[fc]), (nxtks[fc],))
            cur, nxt = nxt, cur
        fin = outA if cur == 0 else outB
        finks = KOpair[cur]
        for fc in range(8):
            act(lambda e, _f=fc, _fv=fin: e.activation(
                _fv[:, _f * BT:(_f + 1) * BT], _fv[:, _f * BT:(_f + 1) * BT], AF.Silu),
                (finks[fc],), (finks[fc],))
        for ti in range(4):
            for fc in range(8):
                bank, bk = psb[4 + (fc % 2)], PS[4 + (fc % 2)]
                tr(bank[:, 0:128], fin[:, fc * BT + ti * 128:fc * BT + (ti + 1) * 128],
                   ident[:], (finks[fc], CONST), bk)
                act(lambda e, _f=fc, _b=bank: e.copy(
                    gstg[:, _f * 128:(_f + 1) * 128], _b[:, 0:128]), (bk,),
                    (f"gstg{fc}", bk))
            p.dma("pool", g_hbm[blk_i * BT + ti * 128:blk_i * BT + (ti + 1) * 128, :],
                  gstg[:], reads=tuple(f"gstg{j}" for j in range(8)), writes=("g_hbm",))
    A32.pop(); A16.pop()
    A32.pop(); A16.pop()
    p.barrier()
    if upto < 4:
        p.finalize_and_emit()
        return nc

    A16.push()
    opt_w = A16.take(16 * 1024)

    p.tagctx = "inproj"
    A32.push(); A16.push()
    winh = A16.take(8 * 2336)
    xsl4 = [A16.take(8 * 512), A16.take(8 * 512)]
    xcv = A32.take(18 * 516)
    zst = A16.take(512)
    cacc = A32.take(512)
    cacc2 = A32.take(512)
    sst = A16.take(512)
    dtp = A32.take(512)
    dve(lambda e: e.memset(xcv[:], 0.0), (), tuple(f"xcv{j}" for j in range(18)))
    for half in range(2):
        ocs = range(0, 16) if half == 0 else range(16, 35)
        off0 = 0 if half == 0 else 2048
        ncols = 2048 if half == 0 else 2336
        for fc in range(8):
            p.dma("pool", winh[:, fc * 2336:fc * 2336 + ncols],
                  hb["WinT"][:, fc * 4384 + off0:fc * 4384 + off0 + ncols],
                  writes=("winh",), tag="winld")
        if half == 0:
            p.dma("pool", opt_w[:], hb["OPT"], writes=("optw",), tag="optld")
        load_xnt(xsl4[0], 0, "ipx0")
        for tg in range(8):
            p.tagctx = f"inproj{half}_{tg}"
            bi = tg % 2
            kxs = f"ipx{bi}"
            if tg + 1 < 8:
                load_xnt(xsl4[1 - bi], tg + 1, f"ipx{1 - bi}")
            t0 = tg * 512
            for oc in ocs:
                bank, bk = psb[oc % 4], PS[oc % 4]
                M = 128 if oc < 34 else 32
                for fc in range(8):
                    mm(bank[0:M, :],
                       winh[:, fc * 2336 + (oc * 128 - off0):fc * 2336 + (oc * 128 - off0) + M],
                       xsl4[bi][:, fc * 512:(fc + 1) * 512], ("winh", kxs), bk,
                       start=(fc == 0), stop=(fc == 7))
                if oc < 16:
                    act(lambda e, _b=bank: e.copy(zst[:], _b[:]), (bk,), ("zst", bk))
                    p.dma("pool", z_hbm[oc * 128:(oc + 1) * 128, t0:t0 + 512], zst[:],
                          reads=("zst",), writes=("z_hbm",))
                elif oc < 34:
                    cc = oc - 16
                    xw_ = xcv[:, cc * 516:(cc + 1) * 516]
                    kcv = f"xcv{cc}"
                    dve(lambda e, _x=xw_: e.tensor_copy(_x[:, 0:3], _x[:, 512:515]),
                        (kcv,), (kcv,))
                    act(lambda e, _x=xw_, _b=bank: e.copy(_x[:, 3:515], _b[:]),
                        (bk, kcv), (kcv, bk))
                    dve(lambda e, _x=xw_, _c=cc: e.tensor_scalar(
                        cacc[:], _x[:, 0:512], convw[:, _c * 4:_c * 4 + 1], None, ALU.mult),
                        (kcv, CONST), ("cacc",))
                    dve(lambda e, _x=xw_, _c=cc: e.scalar_tensor_tensor(
                        cacc2[:], _x[:, 1:513], convw[:, _c * 4 + 1:_c * 4 + 2], cacc[:],
                        ALU.mult, ALU.add), (kcv, CONST, "cacc"), ("cacc2",))
                    dve(lambda e, _x=xw_, _c=cc: e.scalar_tensor_tensor(
                        cacc[:], _x[:, 2:514], convw[:, _c * 4 + 2:_c * 4 + 3], cacc2[:],
                        ALU.mult, ALU.add), (kcv, CONST, "cacc2"), ("cacc",))
                    dve(lambda e, _x=xw_, _c=cc: e.scalar_tensor_tensor(
                        cacc2[:], _x[:, 3:515], convw[:, _c * 4 + 3:_c * 4 + 4], cacc[:],
                        ALU.mult, ALU.add), (kcv, CONST, "cacc"), ("cacc2",))
                    act(lambda e, _c=cc: e.activation(sst[:], cacc2[:], AF.Silu,
                                                      bias=convb[:, _c:_c + 1]),
                        ("cacc2", CONST), ("sst",))
                    if cc < 16:
                        p.dma("pool", xssd_hbm[cc * 128:(cc + 1) * 128, t0:t0 + 512],
                              sst[:], reads=("sst",), writes=("xssd_hbm",))
                    elif cc == 16:
                        p.dma("pool", B_hbm[:, t0:t0 + 512], sst[:],
                              reads=("sst",), writes=("B_hbm",))
                    else:
                        p.dma("pool", C_hbm[:, t0:t0 + 512], sst[:],
                              reads=("sst",), writes=("C_hbm",))
                else:
                    dsl = dtde[0:32, t0:t0 + 512]
                    dve(lambda e, _b=bank, _d=dsl: e.tensor_scalar(
                        _d, _b[0:32, :], dtb_col[:], None, ALU.add), (bk, CONST), ("dtde", bk))
                    act(lambda e, _d=dsl: e.activation(dtp[0:32, :], _d, AF.Abs),
                        ("dtde",), ("dtp",))
                    act(lambda e: e.activation(dtp[0:32, :], dtp[0:32, :], AF.Exp, scale=-1.0),
                        ("dtp",), ("dtp",))
                    act(lambda e: e.activation(dtp[0:32, :], dtp[0:32, :], AF.Ln,
                                               bias=misc[0:32, 1:2]), ("dtp", CONST), ("dtp",))
                    dve(lambda e, _d=dsl: e.tensor_scalar_max(_d, _d, 0.0),
                        ("dtde",), ("dtde",))
                    dve(lambda e, _d=dsl: e.tensor_tensor(_d, _d, dtp[0:32, :], ALU.add),
                        ("dtde", "dtp"), ("dtde",))
    p.tagctx = "dtderived"
    p.barrier()
    A32.pop(); A32.push()
    A16.pop(); A16.push()
    a_sb = A32.take(T)
    acs_sb = A32.take(T)
    res_f = A32.take(T // 2)
    tmp_f = A32.take(T // 2)
    ones32 = A32.take(128)
    bufA = A16.take(T)
    bufB = A16.take(T)
    sb_s = [bufA[0:32, :], bufA[32:64, :], bufA[64:96, :]]
    sb_m = [bufB[0:32, :], bufB[32:64, :], bufB[64:96, :]]
    sb_e = [bufA[96:128, :], bufB[96:128, :]]
    dve(lambda e: e.memset(ones32[:], 1.0), (), ("ones32",))
    dve(lambda e: e.tensor_scalar(a_sb[0:32, :], dtde[0:32, :], nega_col[:], -1.0,
                                  ALU.mult, ALU.mult), ("dtde", CONST), ("a_sb",))
    for c in range(NCH):
        s0 = c * CW
        dve(lambda e, _s=s0: e.tensor_tensor_scan(
            acs_sb[0:32, _s:_s + CW], ones32[0:32, :], a_sb[0:32, _s:_s + CW], 0.0,
            ALU.mult, ALU.add), ("a_sb", "ones32"), ("acs_sb",))
    for c in range(NCH):
        s0 = c * CW
        act(lambda e, _s=s0: e.activation(
            dtde[32:64, _s:_s + CW], acs_sb[0:32, _s:_s + CW], AF.Exp,
            scale=-1.0, bias=acs_sb[0:32, _s + CW - 1:_s + CW]), ("acs_sb",), ("dtde",))
    dve(lambda e: e.tensor_scalar_mul(a_sb[0:32, :], acs_sb[0:32, :], -1.0),
        ("acs_sb",), ("a_sb",))

    H2 = T // 2
    for hh in range(2):
        sl = slice(hh * H2, (hh + 1) * H2)
        act(lambda e, _s=sl: e.copy(sb_s[0][0:32, _s], a_sb[0:32, _s]),
            ("a_sb",), ("spl",))
        act(lambda e, _s=sl: e.copy(tmp_f[0:32, 0:H2], sb_s[0][0:32, _s]),
            ("spl",), ("spt",))
        dve(lambda e, _s=sl: e.tensor_tensor(res_f[0:32, 0:H2], a_sb[0:32, _s],
                                             tmp_f[0:32, 0:H2], ALU.subtract),
            ("a_sb", "spt"), ("spr",))
        act(lambda e, _s=sl: e.copy(sb_s[1][0:32, _s], res_f[0:32, 0:H2]),
            ("spr",), ("spl",))
        act(lambda e, _s=sl: e.copy(tmp_f[0:32, 0:H2], sb_s[1][0:32, _s]),
            ("spl",), ("spt",))
        dve(lambda e: e.tensor_tensor(res_f[0:32, 0:H2], res_f[0:32, 0:H2],
                                      tmp_f[0:32, 0:H2], ALU.subtract),
            ("spr", "spt"), ("spr",))
        act(lambda e, _s=sl: e.copy(sb_s[2][0:32, _s], res_f[0:32, 0:H2]),
            ("spr",), ("spl",))
    for k in range(3):
        dve(lambda e, _k=k: e.tensor_scalar_mul(sb_m[_k][0:32, :], sb_s[_k][0:32, :],
                                                -1.0), ("spl",), ("spm",))
    act(lambda e: e.activation(a_sb[0:32, :], acs_sb[0:32, :], AF.Exp),
        ("acs_sb", "a_sb"), ("a_sb",))
    for hh in range(2):
        sl = slice(hh * H2, (hh + 1) * H2)
        act(lambda e, _s=sl: e.copy(sb_e[0][0:32, _s], a_sb[0:32, _s]),
            ("a_sb",), ("spl",))
        act(lambda e, _s=sl: e.copy(tmp_f[0:32, 0:H2], sb_e[0][0:32, _s]),
            ("spl",), ("spt",))
        dve(lambda e, _s=sl: e.tensor_tensor(res_f[0:32, 0:H2], a_sb[0:32, _s],
                                             tmp_f[0:32, 0:H2], ALU.subtract),
            ("a_sb", "spt"), ("spr",))
        act(lambda e, _s=sl: e.copy(sb_e[1][0:32, _s], res_f[0:32, 0:H2]),
            ("spr",), ("spl",))
    for k in range(3):
        p.dma("pool", s123_hbm[k * 32:(k + 1) * 32, :], sb_s[k][0:32, :],
              reads=("spl",), writes=("s123_hbm",))
        p.dma("pool", m123_hbm[k * 32:(k + 1) * 32, :], sb_m[k][0:32, :],
              reads=("spm",), writes=("m123_hbm",))
    for k in range(2):
        p.dma("pool", e12_hbm[k * 32:(k + 1) * 32, :], sb_e[k][0:32, :],
              reads=("spl",), writes=("e12_hbm",))
    A32.pop(); A16.pop()
    p.barrier()
    if upto < 5:
        p.finalize_and_emit()
        return nc

    p.tagctx = "ssd"
    A32.push(); A16.push()
    hstate_bf = A16.take(2048)
    xsdL = [A16.take(2048), A16.take(2048)]
    BL = [A16.take(128), A16.take(128)]
    CL = [A16.take(128), A16.take(128)]
    statb = A16.take(4096)
    movb = A16.take(4096)
    erows = A16.take(4096)
    zb = A32.take(2048)
    xgb = A32.take(2048)
    xtok = A16.take(2048)
    xdt = A16.take(2048)
    xw_t = A16.take(2048)
    btok = A16.take(128)
    mpair = A16.take(512)
    csc_all = A16.take(4096)
    ynb = xdt
    szb = xw_t
    cbt_sb = A32.take(128)
    dcst = A32.take(512)
    sctok = A32.take(64)
    y_sb = A32.take(2048)
    gy = A32.take(2048)
    sz = A32.take(2048)
    nrow = A32.take(128)
    x1st = A32.take(1024)
    dve(lambda e: e.memset(hstate[:], 0.0), (), ("hstate",), "hinit")

    def ssd_loads(c, bi):
        t0 = c * CW
        kx, kb, kc_ = f"xsd{bi}", f"Bb{bi}", f"Cb{bi}"
        p.dma("sp", xsdL[bi][:].rearrange("p (q t) -> p q t", q=16),
              xssd_hbm[:, t0:t0 + CW].rearrange("(q p) t -> p q t", p=128), writes=(kx,))
        p.dma("sp", BL[bi][:], B_hbm[:, t0:t0 + CW], writes=(kb,))
        p.dma("sp", CL[bi][:], C_hbm[:, t0:t0 + CW], writes=(kc_,))

    def flat_loads(c):
        t0 = c * CW
        for k in range(3):
            p.dma("sp", statb[k:k + 1, :].rearrange("p (h t) -> p h t", h=32),
                  s123_hbm[k * 32:(k + 1) * 32, t0:t0 + CW].unsqueeze(0), writes=("fl",))
            p.dma("sp", movb[3 + k:4 + k, :].rearrange("p (h t) -> p h t", h=32),
                  m123_hbm[k * 32:(k + 1) * 32, t0:t0 + CW].unsqueeze(0), writes=("fl",))
        for k in range(2):
            p.dma("sp", erows[k:k + 1, :].rearrange("p (h t) -> p h t", h=32),
                  e12_hbm[k * 32:(k + 1) * 32, t0:t0 + CW].unsqueeze(0), writes=("fl",))

    dve(lambda e: e.memset(statb[0:32, :], 1.0), (), ("fl",))
    dve(lambda e: e.memset(movb[0:32, :], 1.0), (), ("fl",))
    ssd_loads(0, 0)
    for cc in range(NCH):
        cb = cc % 2
        t0 = cc * CW
        p.tagctx = f"ssd{cc}"
        if cc + 1 < NCH:
            ssd_loads(cc + 1, 1 - cb)
        kx, kb, kc_, kfl = f"xsd{cb}", f"Bb{cb}", f"Cb{cb}", "fl"
        kz, kxg = "zb", "xgb"
        flat_loads(cc)
        p.dma("pool", zb[:].rearrange("p (q t) -> p q t", q=16),
              z_hbm[:, t0:t0 + CW].rearrange("(q p) t -> p q t", p=128), writes=(kz,))
        p.dma("pool", xgb[:, 0:1024], g_hbm[t0:t0 + CW, :], writes=(kxg,))
        p.dma("sp", xgb[:, 1024:2048], x_in[t0:t0 + CW, :], writes=(kxg,))
        def sctr(e, _t0=t0):
            return e.transpose(psb[5][0:128, 0:64], dtde[0:64, _t0:_t0 + CW],
                               ident[0:64, 0:64])
        p.op("pe", sctr, reads=("dtde", CONST, PS[5]), writes=(PS[5],), tag="sctr")
        act(lambda e: e.copy(sctok[:], psb[5][:, 0:64]), (PS[5],), ("sctok", PS[5]))
        tr(psbb[0][:, 0:128], BL[cb][:], identb[:], (kb, CONST), PSB[0])
        act(lambda e: e.copy(btok[:], psbb[0][:, 0:128]), (PSB[0],), ("btok", PSB[0]))
        for q in range(16):
            pk, pb_ = PSB[q % 2], psbb[q % 2]
            tr(pb_[:, 0:128], xsdL[cb][:, q * 128:(q + 1) * 128], identb[:], (kx, CONST), pk)
            act(lambda e, _q=q, _pb=pb_: e.copy(
                xtok[:, _q * 128:(_q + 1) * 128], _pb[:, 0:128]), (pk,),
                (f"xtok{q}", pk))
        dve(lambda e: e.tensor_tensor(
            xdt[:].rearrange("p (h j) -> p h j", h=32),
            xtok[:].rearrange("p (h j) -> p h j", h=32),
            sctok[:, 0:32].unsqueeze(2).broadcast_to([128, 32, 64]), ALU.mult),
            tuple(f"xtok{j}" for j in range(16)) + ("sctok",), ("xdt",))
        dve(lambda e: e.tensor_tensor(
            xw_t[:].rearrange("p (h j) -> p h j", h=32),
            xdt[:].rearrange("p (h j) -> p h j", h=32),
            sctok[:, 32:64].unsqueeze(2).broadcast_to([128, 32, 64]), ALU.mult),
            ("xdt", "sctok"), ("xw",))
        dve(lambda e: e.tensor_copy(hstate_bf[:], hstate[:]), ("hstate",), ("hbf",))
        mm(psb[3][:, 0:128], BL[cb][:], CL[cb][:], (kb, kc_), PS[3], tag="cbt")
        dve(lambda e: e.tensor_tensor(cbt_sb[:], psb[3][:, 0:128], maskT[:], ALU.mult),
            (PS[3], CONST), ("cbt", PS[3]))
        if cc > 0:
            for h4 in range(8):
                mm(psb[3][:], movb[0:2, 0:128],
                   erows[0:2, h4 * 512:(h4 + 1) * 512], (kfl,), PS[3], tag="eacsB")
                dve(lambda e, _h=h4, _c=CL[cb]: e.tensor_tensor(
                    csc_all[:, _h * 512:(_h + 1) * 512].rearrange("p (g t) -> p g t", g=4),
                    _c[:].unsqueeze(1).broadcast_to([128, 4, 128]),
                    psb[3][:].rearrange("p (g t) -> p g t", g=4),
                    ALU.mult), (kc_, PS[3]), (f"csca{h4}", PS[3]))
        def mchain(pr):
            par = pr % 2
            h0, h1_ = 2 * pr, 2 * pr + 1
            for hi, hh in enumerate((h0, h1_)):
                dbk = PS[2 + hi]
                mm(psb[2 + hi][:, 0:128], statb[0:6, hh * 128:(hh + 1) * 128],
                   movb[0:6, hh * 128:(hh + 1) * 128], (kfl,), dbk,
                   tag="diff")
                act(lambda e, _h=hi, _p=par, _b=psb[2 + hi]: e.activation(
                    dcst[:, (_p * 2 + _h) * 128:(_p * 2 + _h + 1) * 128], _b[:, 0:128],
                    AF.Relu, scale=-1.0), (dbk,), (f"dcst{par}{hi}", dbk))
                act(lambda e, _h=hi, _p=par: e.activation(
                    dcst[:, (_p * 2 + _h) * 128:(_p * 2 + _h + 1) * 128],
                    dcst[:, (_p * 2 + _h) * 128:(_p * 2 + _h + 1) * 128],
                    AF.Exp, scale=-1.0), (f"dcst{par}{hi}",), (f"dcst{par}{hi}",))
                dve(lambda e, _h=hi, _p=par: e.tensor_tensor(
                    mpair[:, (_p * 2 + _h) * 128:(_p * 2 + _h + 1) * 128],
                    dcst[:, (_p * 2 + _h) * 128:(_p * 2 + _h + 1) * 128],
                    cbt_sb[:], ALU.mult), (f"dcst{par}{hi}", "cbt"), (f"mpair{par}{hi}",))

        def ymms(pr):
            par = pr % 2
            h0, h1_ = 2 * pr, 2 * pr + 1
            ypk, ypb = PS[pr % 2], psb[pr % 2]
            mm(ypb[0:64, 0:128], xdt[:, h0 * 64:(h0 + 1) * 64],
               mpair[:, (par * 2) * 128:(par * 2 + 1) * 128],
               ("xdt", f"mpair{par}0"), ypk, start=True, stop=(cc == 0), tag="y1a")
            if cc > 0:
                mm(ypb[0:64, 0:128], hstate_bf[:, h0 * 64:(h0 + 1) * 64],
                   csc_all[:, h0 * 128:(h0 + 1) * 128],
                   ("hbf", f"csca{h0 // 4}"), ypk, start=False, stop=True, tag="y2a")
            mm(ypb[64:128, 0:128], xdt[:, h1_ * 64:(h1_ + 1) * 64],
               mpair[:, (par * 2 + 1) * 128:(par * 2 + 2) * 128],
               ("xdt", f"mpair{par}1"), ypk, start=True, stop=(cc == 0), tp=(0, 64),
               tag="y1b")
            if cc > 0:
                mm(ypb[64:128, 0:128], hstate_bf[:, h1_ * 64:(h1_ + 1) * 64],
                   csc_all[:, h1_ * 128:(h1_ + 1) * 128],
                   ("hbf", f"csca{h1_ // 4}"), ypk, start=False, stop=True,
                   tp=(0, 64), tag="y2b")
            spk, spb = PS[4 + (pr % 2)], psb[4 + (pr % 2)]
            mm(spb[:, 0:64], btok[:], xw_t[:, h0 * 64:(h0 + 1) * 64], ("btok", "xw"), spk,
               start=True, stop=False, tag="sta")
            mm(spb[:, 64:128], btok[:], xw_t[:, h1_ * 64:(h1_ + 1) * 64], ("btok", "xw"),
               spk, start=False, stop=True, tag="stb")
            act(lambda e, _p=pr, _b=spb: e.copy(
                states_sb[:, _p * 128:(_p + 1) * 128], _b[:, 0:128]), (spk,),
                (f"states{pr}", spk))
            def yev(e, _p=pr, _b=ypb, _x=xsdL[cb]):
                return e.scalar_tensor_tensor(
                    y_sb[:, _p * 128:(_p + 1) * 128], _x[:, _p * 128:(_p + 1) * 128],
                    dcols[:, _p:_p + 1], _b[:, 0:128], ALU.mult, ALU.add)
            dve(yev, (kx, CONST, ypk), (f"y_sb{pr}", ypk))

        mchain(0)
        for pr in range(16):
            if pr + 1 < 16:
                mchain(pr + 1)
            ymms(pr)
        mm(psb[2][:, 0:32], movb[0:2, 0:128], erows[0:2, 127:4096:128],
           (kfl,), PS[2], tag="cdecayB")
        dve(lambda e: e.tensor_tensor(
            hstate[:].rearrange("p (h j) -> p h j", h=32),
            hstate[:].rearrange("p (h j) -> p h j", h=32),
            psb[2][:, 0:32].unsqueeze(2).broadcast_to([128, 32, 64]), ALU.mult),
            ("hstate", PS[2]), ("hstate", PS[2]))
        dve(lambda e: e.tensor_tensor(hstate[:], hstate[:], states_sb[:], ALU.add),
            ("hstate",) + tuple(f"states{j}" for j in range(16)), ("hstate",))
        if ydbg_hbm is not None:
            p.dma("pool", ydbg_hbm[:, t0:t0 + CW].rearrange("(q p) t -> p q t", p=128),
                  y_sb[:].rearrange("p (q t) -> p q t", q=16),
                  reads=tuple(f"y_sb{j}" for j in range(16)), writes=("ydbg",))
        act(lambda e: e.activation(sz[:], zb[:], AF.Silu), (kz,), ("sz",))
        dve(lambda e: e.tensor_tensor(gy[:], y_sb[:], sz[:], ALU.mult),
            tuple(f"y_sb{j}" for j in range(16)) + ("sz",), ("gy",))
        act(lambda e: e.activation(szb[:], gy[:], AF.Square), ("gy", "xw"), ("xw",))
        for q in range(16):
            mm(psb[3][0:1, 0:128], onesbc[:], szb[:, q * 128:(q + 1) * 128],
               (CONST, "xw"), PS[3], start=(q == 0), stop=(q == 15), tag="nrm")
        dve(lambda e: e.tensor_scalar(nrow[0:1, :], psb[3][0:1, 0:128], 1.0 / 2048,
                                      misc[0:1, 0:1], ALU.mult, ALU.add),
            (PS[3],), ("nrow", PS[3]), sreads=(CONST,))
        act(lambda e: e.activation(nrow[0:1, :], nrow[0:1, :], AF.Ln), ("nrow",), ("nrow",))
        act(lambda e: e.activation(nrow[0:1, :], nrow[0:1, :], AF.Exp, scale=-0.5),
            ("nrow",), ("nrow",))
        mm(psb[3][:, 0:128], onesrow[:], nrow[0:1, :], ("nrow",), PS[3], tag="rstdB")
        dve(lambda e: e.tensor_tensor(
            gy[:].rearrange("p (q t) -> p q t", q=16),
            gy[:].rearrange("p (q t) -> p q t", q=16),
            psb[3][:, 0:128].unsqueeze(1).broadcast_to([128, 16, 128]), ALU.mult),
            ("gy", PS[3]), ("gy", PS[3]))
        dve(lambda e: e.tensor_tensor(
            ynb[:].rearrange("p (q t) -> p q t", q=16),
            gy[:].rearrange("p (q t) -> p q t", q=16),
            normw[:].unsqueeze(2).broadcast_to([128, 16, 128]), ALU.mult),
            ("gy", CONST), ("xdt",))
        if gydbg_hbm is not None:
            p.dma("pool", gydbg_hbm[:, t0:t0 + CW].rearrange("(q p) t -> p q t", p=128),
                  ynb[:].rearrange("p (q t) -> p q t", q=16),
                  reads=("ynb",), writes=("gydbg",))
        for og in range(2):
            bank, bk = psb[4 + og], PS[4 + og]
            for q in range(16):
                mm(bank[:], ynb[:, q * 128:(q + 1) * 128],
                   opt_w[:, q * 1024 + og * 512:q * 1024 + (og + 1) * 512],
                   ("xdt", "optw"), bk, start=(q == 0), stop=(q == 15))
            def comb1(e, _b=bank, _og=og):
                return e.tensor_tensor(x1st[:, _og * 512:(_og + 1) * 512], _b[:],
                                       xgb[:, _og * 512:(_og + 1) * 512], ALU.mult)
            dve(comb1, (bk, kxg), ("x1st", bk))
            def comb2(e, _og=og):
                return e.tensor_tensor(x1st[:, _og * 512:(_og + 1) * 512],
                                       x1st[:, _og * 512:(_og + 1) * 512],
                                       xgb[:, 1024 + _og * 512:1024 + (_og + 1) * 512],
                                       ALU.add)
            dve(comb2, ("x1st", kxg), ("x1st",))
        p.dma("pool", x1_hbm[t0:t0 + CW, :], x1st[:], reads=("x1st",), writes=("x1_hbm",))
    A32.pop(); A16.pop()
    A16.pop()
    p.barrier()
    if upto < 6:
        p.finalize_and_emit()
        return nc

    phase_LN(x1_hbm, 1)
    p.barrier()
    if upto < 7:
        p.finalize_and_emit()
        return nc

    p.tagctx = "qkv"
    A32.push(); A16.push()
    wq = A16.take(8 * 1536)
    cos2 = A32.take(4096)
    sin2 = A32.take(4096)
    p.dma("sp", cos2[:], hb["cos2"], writes=("cs",))
    p.dma("sp", sin2[:], hb["sin2"], writes=("cs",))
    xsl7 = [A16.take(8 * 512), A16.take(8 * 512)]
    tm1b = [A32.take(512), A32.take(512)]
    obfb = [A16.take(512), A16.take(512)]
    rbfb = [A16.take(512), A16.take(512)]
    for qhalf in range(2):
      for fc in range(8):
          p.dma("pool", wq[:, fc * 1536:(fc + 1) * 1536],
                hb["WqkvP"][:, fc * 3072 + qhalf * 1536:fc * 3072 + (qhalf + 1) * 1536],
                writes=("wq",))
      load_xnt(xsl7[0], 0, "qx0")
      for tg in range(8):
        p.tagctx = f"qkv{qhalf}_{tg}"
        bi = tg % 2
        kxs = f"qx{bi}"
        if tg + 1 < 8:
            load_xnt(xsl7[1 - bi], tg + 1, f"qx{1 - bi}")
        for oci in range(12):
            oc = qhalf * 12 + oci
            bank, bk = psb[oc % 4], PS[oc % 4]
            for fc in range(8):
                mm(bank[:], wq[:, fc * 1536 + oci * 128:fc * 1536 + (oci + 1) * 128],
                   xsl7[bi][:, fc * 512:(fc + 1) * 512], ("wq", kxs), bk,
                   start=(fc == 0), stop=(fc == 7))
            par = oc % 2
            obf, kob = obfb[par], f"obf{par}"
            if oc < 16:
                tm1, ktm = tm1b[par], f"tm1{par}"
                tm2, ktm2 = psb[4 + par], PS[4 + par]
                rbf, krb = rbfb[par], f"rbf{par}"
                act(lambda e, _b=bank, _o=obf: e.copy(_o[:], _b[:]), (bk,), (kob, bk))
                dve(lambda e, _b=bank, _t=tg, _m=tm1: e.tensor_tensor(
                    _m[:], _b[:], cos2[:, _t * 512:(_t + 1) * 512], ALU.mult),
                    (bk, "cs"), (ktm, bk))
                dve(lambda e, _b=bank, _t=tg, _m=tm2: e.tensor_tensor(
                    _m[:], _b[:], sin2[:, _t * 512:(_t + 1) * 512], ALU.mult),
                    (bk, "cs", ktm2), (ktm2, bk))
                for bb in (0, 64):
                    dve(lambda e, _b=bb, _m=tm1, _m2=tm2, _r=rbf: e.tensor_tensor(
                        _r[_b:_b + 32, :], _m[_b:_b + 32, :], _m2[_b + 32:_b + 64, :],
                        ALU.subtract), (ktm, ktm2), (krb,))
                    dve(lambda e, _b=bb, _m=tm1, _m2=tm2, _r=rbf: e.tensor_tensor(
                        _r[_b + 32:_b + 64, :], _m2[_b:_b + 32, :],
                        _m[_b + 32:_b + 64, :],
                        ALU.add), (ktm, ktm2), (krb,))
                dst_p = q_hbm if oc < 8 else k_hbm
                dst_r = qr_hbm if oc < 8 else kr_hbm
                ro = (oc % 8) * 128
                p.dma("pool", dst_p[ro:ro + 128, tg * 512:(tg + 1) * 512], obf[:],
                      reads=(kob,), writes=("qk_hbm",))
                p.dma("pool", dst_r[ro:ro + 128, tg * 512:(tg + 1) * 512], rbf[:],
                      reads=(krb,), writes=("qk_hbm",))
            else:
                act(lambda e, _b=bank, _o=obf: e.copy(_o[:], _b[:]), (bk,), (kob, bk))
                ro = (oc - 16) * 128
                p.dma("pool", v_hbm[ro:ro + 128, tg * 512:(tg + 1) * 512], obf[:],
                      reads=(kob,), writes=("v_hbm",))
    A32.pop(); A16.pop()
    p.barrier()
    if upto < 8:
        p.finalize_and_emit()
        return nc

    p.tagctx = "l1w1"
    A32.push(); A16.push()
    w1h = A16.take(8 * 2048)
    xsl8 = [A16.take(8 * 512), A16.take(8 * 512)]
    abf = A16.take(4096)
    ev1 = A16.take(1024)
    od1 = A16.take(32)
    od2 = A16.take(32)
    hst = A16.take(512)
    b1cs = A32.take(32)
    ev1f = A32.take(1024)
    cst2 = A32.take(512)
    ev2f = A32.take(1024)
    z16 = A32.take(1024)
    for fc in range(8):
        p.dma("act", od2[:, fc * 4:fc * 4 + 4],
              xnt_hbm[fc * 128:(fc + 1) * 128, 10:17:2], writes=("od2",),
              allow_slow_non_contiguous=True)
    p.dma("sp", b1cs[:], hb["b1c"], writes=("b1c",))
    for i in range(4):
        p.dma("sp", ev1f[32 * i:32 * i + 1, :], xrow_hbm[1 + 2 * i:2 + 2 * i, :],
              writes=("ev1f",))
        dve(lambda e, _i=i: e.tensor_copy(ev1[32 * _i:32 * _i + 1, :],
                                          ev1f[32 * _i:32 * _i + 1, :]),
            ("ev1f",), ("ev1",))
    for fc in range(8):
        p.dma("act", od1[:, fc * 4:fc * 4 + 4],
              xnt_hbm[fc * 128:(fc + 1) * 128, 2:9:2], writes=("od1",),
              allow_slow_non_contiguous=True)
    p.tagctx = "e2st"
    dve(lambda e: e.memset(z16[:], 0.0), (), ("z16",))
    p.dma("act", e2_hbm, z16[0:16, :], reads=("z16",), writes=("e2_hbm",))
    for i in range(4):
        p.dma("act", ev2f[32 * i:32 * i + 1, :], xrow_hbm[9 + 2 * i:10 + 2 * i, :],
              writes=("ev2f",))
    for qq in range(4):
        for i in range(4):
            p.dma("act", e2_hbm[qq * 4 + i:qq * 4 + i + 1, i * 256:(i + 1) * 256],
                  ev2f[32 * i:32 * i + 1, qq:qq + 1021:4], reads=("ev2f", "e2_hbm"),
                  writes=("e2_hbm",), allow_slow_non_contiguous=True)
    p.tagctx = "l1w1"
    load_xnt(xsl8[0], 0, "wx0")
    for tg in range(8):
        p.tagctx = f"l1a{tg}"
        bi = tg % 2
        kxs = f"wx{bi}"
        if tg + 1 < 8:
            load_xnt(xsl8[1 - bi], tg + 1, f"wx{1 - bi}")
        for i in range(4):
            bank, bk = psb[4 + i % 2], PS[4 + i % 2]
            for fc in range(8):
                mm(bank[0:1, :], od1[:, fc * 4 + i:fc * 4 + i + 1],
                   xsl8[bi][:, fc * 512:(fc + 1) * 512], ("od1", kxs), bk,
                   start=(fc == 0), stop=(fc == 7), tag="apass")
            def aev(e, _i=i, _tg=tg, _b=bank):
                return e.tensor_scalar(abf[32 * _i:32 * _i + 1, _tg * 512:(_tg + 1) * 512],
                                       _b[0:1, :], rowc[0:1, 3:4], None, ALU.mult)
            dve(aev, (bk, CONST), ("abf", bk))
    for half in range(2):
        p.tagctx = f"l1w1h{half}"
        for fc in range(8):
            p.dma("pool", w1h[:, fc * 2048:(fc + 1) * 2048],
                  hb["W1T"][:, fc * 4096 + half * 2048:fc * 4096 + (half + 1) * 2048],
                  writes=("w1h",), tag="w1ld")
        load_xnt(xsl8[0], 0, "wx0")
        for tg in range(8):
            p.tagctx = f"l1w1h{half}t{tg}"
            bi = tg % 2
            kxs = f"wx{bi}"
            if tg + 1 < 8:
                load_xnt(xsl8[1 - bi], tg + 1, f"wx{1 - bi}")
            for oci in range(16):
                oc = half * 16 + oci
                bank, bk = psb[oci % 4], PS[oci % 4]
                for fc in range(8):
                    mm(bank[:], w1h[:, fc * 2048 + oci * 128:fc * 2048 + (oci + 1) * 128],
                       xsl8[bi][:, fc * 512:(fc + 1) * 512], ("w1h", kxs), bk,
                       start=(fc == 0), stop=False)
                i_l = oc // 8
                mm(bank[:], ev1[32 * i_l:32 * i_l + 1, (oc % 8) * 128:(oc % 8 + 1) * 128],
                   abf[32 * i_l:32 * i_l + 1, tg * 512:(tg + 1) * 512],
                   ("ev1", "abf"), bk, start=False, stop=True, tag="lora1",
                   tp=((96, 0) if i_l == 3 else None))
                act(lambda e, _o=oc, _b=bank: e.activation(
                    hst[:], _b[:], AF.Gelu, bias=b1cs[:, _o:_o + 1]),
                    (bk, "b1c"), ("hst", bk))
                p.dma("pool", h_hbm[oc * 128:(oc + 1) * 128, tg * 512:(tg + 1) * 512],
                      hst[:], reads=("hst",), writes=("h_hbm",))
                jq = oc // 8
                cbank, cpk = psb[4 + (jq % 2)], PS[4 + (jq % 2)]
                mm(cbank[0:4, :], od2[:, (oc % 8) * 4:(oc % 8) * 4 + 4], hst[:],
                   ("od2", "hst"), cpk, start=(oc % 8 == 0), stop=(oc % 8 == 7),
                   tag="cacc2")
                if oc % 8 == 7:
                    dve(lambda e, _cb=cbank: e.tensor_scalar(
                        cst2[0:4, :], _cb[0:4, :], misc[0:4, 2:3], None, ALU.mult),
                        (cpk,), ("cst2", cpk), sreads=(CONST,))
                    p.dma("pool", c_hbm[0:4, jq * 4096 + tg * 512:
                                        jq * 4096 + (tg + 1) * 512],
                          cst2[0:4, :], reads=("cst2",), writes=("c_hbm",))
    A32.pop(); A16.pop()
    p.barrier()
    if upto < 9:
        p.finalize_and_emit()
        return nc

    p.tagctx = "attn"
    A32.push(); A16.push()
    amk = A32.take(2048)
    denc = A32.take(32)
    rdts = A32.take(128)
    rdtf = A32.take(128)
    rdb = A32.take(512)
    for mi in range(4):
        p.dma("sp", amk[:, mi * 512:(mi + 1) * 512], hb[f"amask{mi}"], writes=("amk",))
    A16.push()
    qsl = [A16.take(4096), A16.take(4096)]
    ksl = [A16.take(4096), A16.take(4096)]
    dumeb = [A16.take(4096), A16.take(4096), A16.take(4096)]
    rdt1 = A16.take(128)
    rdt2 = A16.take(128)

    def dload(h, bi):
        r0 = h * 64
        p.dma("sp", qsl[bi][0:64, :], q_hbm[r0:r0 + 64, :], writes=(f"qs{bi}",))
        p.dma("sp", ksl[bi][0:64, :], k_hbm[r0:r0 + 64, :], writes=(f"ks{bi}",))

    dload(0, 0)
    for hh in range(AH):
        p.tagctx = f"atd{hh}"
        bi = hh % 2
        if hh + 1 < AH:
            dload(hh + 1, 1 - bi)
        kq, kk = f"qs{bi}", f"ks{bi}"
        for it in range(32):
            du = dumeb[it % 3]
            kdu = f"du{it % 3}"
            for jp in range(4):
                pi = (4 * it + jp) % 3
                k0, k1 = PS[2 * pi], PS[2 * pi + 1]
                mm(psb[2 * pi][:], qsl[bi][0:64, it * 128:(it + 1) * 128],
                   ksl[bi][0:64, (2 * jp) * 512:(2 * jp + 1) * 512], (kq, kk), k0,
                   tag="den")
                mm(psb[2 * pi + 1][:], qsl[bi][0:64, it * 128:(it + 1) * 128],
                   ksl[bi][0:64, (2 * jp + 1) * 512:(2 * jp + 2) * 512], (kq, kk), k1,
                   tag="den")
                act(lambda e, _pi=pi, _jp=jp, _du=du: e.activation(
                    _du[:, _jp * 1024:(_jp + 1) * 1024], pspair[_pi][:], AF.Exp,
                    scale=0.125), (k0, k1, kdu), (kdu, k0, k1))
            dve(lambda e, _it=it, _du=du: e.reduce_sum(
                denc[:, _it:_it + 1], _du[:], axis=AX.X), (kdu,), ("denc", kdu))
        dve(lambda e: e.reciprocal(denc[:], denc[:]), ("denc",), ("denc",))
        tr(psb[5][0:32, 0:128], denc[:], ident[:], ("denc", CONST), PS[5], tag="dtr")
        dve(lambda e: e.tensor_copy(rdts[0:32, :], psb[5][0:32, 0:128]),
            (PS[5],), ("rdt", PS[5]))
        dve(lambda e: e.tensor_copy(rdt1[0:32, :], rdts[0:32, :]), ("rdt",), ("rd1",))
        dve(lambda e: e.tensor_copy(rdtf[0:32, :], rdt1[0:32, :]), ("rd1",), ("rdf",))
        dve(lambda e: e.tensor_tensor(rdtf[0:32, :], rdts[0:32, :], rdtf[0:32, :],
                                      ALU.subtract), ("rdt", "rdf"), ("rdf",))
        dve(lambda e: e.tensor_copy(rdt2[0:32, :], rdtf[0:32, :]), ("rdf",), ("rd2",))
        p.dma("pool", rden_hbm[2 * hh:2 * hh + 1, :].rearrange("r (a b) -> (r a) b", a=32),
              rdt1[0:32, :], reads=("rd1",), writes=("rden_hbm",))
        p.dma("pool", rden_hbm[2 * hh + 1:2 * hh + 2, :].rearrange("r (a b) -> (r a) b", a=32),
              rdt2[0:32, :], reads=("rd2",), writes=("rden_hbm",))
    A16.pop()
    p.barrier()
    A16.push()
    qrl = [A16.take(4096), A16.take(4096)]
    krl = [A16.take(4096), A16.take(4096)]
    vfl = [A16.take(4096), A16.take(4096)]
    vtok = A16.take(2048)
    attnT = A16.take(4096)
    rdf16 = [A16.take(4096), A16.take(4096)]
    ybst = A16.take(512)

    def nload(h, bi):
        r0 = h * 64
        p.dma("sp", qrl[bi][0:64, :], qr_hbm[r0:r0 + 64, :], writes=(f"qr{bi}",))
        p.dma("sp", krl[bi][0:64, :], kr_hbm[r0:r0 + 64, :], writes=(f"kr{bi}",))
        p.dma("sp", vfl[bi][0:64, :], v_hbm[r0:r0 + 64, :], writes=(f"vf{bi}",))
        p.dma("sp", rdf16[bi][0:2, :], rden_hbm[2 * h:2 * h + 2, :],
              writes=(f"rdf{bi}",))

    nload(0, 0)
    for hh in range(AH):
        p.tagctx = f"at{hh}"
        bi = hh % 2
        r0 = hh * 64
        if hh + 1 < AH:
            nload(hh + 1, 1 - bi)
        kq, kk, kv, krd = f"qr{bi}", f"kr{bi}", f"vf{bi}", f"rdf{bi}"
        for jt in range(32):
            pk, pb_ = PSB[jt % 2], psbb[jt % 2]
            tr(pb_[:, 0:64], vfl[bi][0:64, jt * 128:(jt + 1) * 128],
               identb[0:64, 0:64], (kv, CONST), pk, tag="vtr")
            dve(lambda e, _j=jt, _pb=pb_: e.tensor_copy(
                vtok[:, _j * 64:(_j + 1) * 64], _pb[:, 0:64]), (pk,), (f"vtok{jt}", pk))
        for ig in range(8):
            njt = 4 * ig + 4

            def avmm(jt, _njt=njt):
                mm(psb[4][0:64, :], vtok[:, jt * 64:(jt + 1) * 64],
                   attnT[:, (jt % 8) * 512:(jt % 8 + 1) * 512],
                   (f"vtok{jt}", f"aT{jt % 8}"), PS[4],
                   start=(jt == 0), stop=(jt == _njt - 1), tag="av")

            for jt2 in range(njt // 2):
                pi = jt2 % 2
                k0, k1 = PS[2 * pi], PS[2 * pi + 1]
                jta, jtb = 2 * jt2, 2 * jt2 + 1
                sa = jta % 8
                mm(psb[2 * pi][:], krl[bi][0:64, jta * 128:(jta + 1) * 128],
                   qrl[bi][0:64, ig * 512:(ig + 1) * 512], (kk, kq), k0, tag="numm")
                mm(psb[2 * pi + 1][:], krl[bi][0:64, jtb * 128:(jtb + 1) * 128],
                   qrl[bi][0:64, ig * 512:(ig + 1) * 512], (kk, kq), k1, tag="numm")
                act(lambda e, _pi=pi, _s=sa: e.activation(
                    attnT[:, _s * 512:(_s + 2) * 512], pspair[_pi][:], AF.Exp,
                    scale=0.125), (k0, k1, f"aT{sa}", f"aT{sa + 1}"),
                    (f"aT{sa}", f"aT{sa + 1}", k0, k1))
                for jt, mi in ((jta, jta - 4 * ig), (jtb, jtb - 4 * ig)):
                    if mi >= 0:
                        dve(lambda e, _s=jt % 8, _m=mi: e.tensor_tensor(
                            attnT[:, _s * 512:(_s + 1) * 512],
                            attnT[:, _s * 512:(_s + 1) * 512],
                            amk[:, _m * 512:(_m + 1) * 512], ALU.mult),
                            (f"aT{jt % 8}", "amk"), (f"aT{jt % 8}",))
                if jt2 >= 2:
                    avmm(2 * jt2 - 4)
                    avmm(2 * jt2 - 3)
            for jt in range(max(0, njt - 4), njt):
                avmm(jt)
            mm(psb[5][:], ones2b[:, 0:128], rdf16[bi][0:2, ig * 512:(ig + 1) * 512],
               (CONST, krd), PS[5], tag="rdb")
            dve(lambda e: e.tensor_copy(rdb[0:64, :], psb[5][0:64, :]), (PS[5],),
                ("rdb", PS[5]))
            dve(lambda e: e.tensor_tensor(ybst[0:64, :], psb[4][0:64, :], rdb[0:64, :],
                                          ALU.mult), (PS[4], "rdb"), ("ybst", PS[4]))
            p.dma("pool", y_hbm[r0:r0 + 64, ig * 512:(ig + 1) * 512], ybst[0:64, :],
                  reads=("ybst",), writes=("y_hbm",))
    A16.pop()
    A32.pop(); A16.pop()
    p.barrier()
    if upto < 11:
        p.finalize_and_emit()
        return nc

    A32.push(); A16.push()
    c16 = A16.take(4096)
    e2b = A16.take(1024)
    for qq in range(4):
        p.dma("pool", c16[qq * 4:(qq + 1) * 4, :],
              c_hbm[:, qq * T:(qq + 1) * T], writes=("c16",))
    p.dma("pool", e2b[0:16, :], e2_hbm, writes=("c16",))

    p.tagctx = "s1"
    A32.push(); A16.push()
    wo = A16.take(8 * 1024)
    p.dma("pool", wo[:], hb["Wout"], writes=("wo",))
    boutb = A16.take(1024)
    boutf = A32.take(1024)
    p.dma("sp", boutf[0:1, :], hb["bout"], writes=("boutf",))
    dve(lambda e: e.tensor_copy(boutb[0:1, :], boutf[0:1, :]), ("boutf",), ("boutb",))
    ysl = [A16.take(4096), A16.take(4096)]
    s1st = A32.take(1024)

    def yld(t4, bi):
        p.dma("sp", ysl[bi][:].rearrange("p (c t) -> p c t", c=8),
              y_hbm.rearrange("(c p) t -> p c t", p=128)
              [:, :, t4 * 512:(t4 + 1) * 512], writes=(f"ysl{bi}",))

    yld(0, 0)
    for t4 in range(8):
        bi = t4 % 2
        if t4 + 1 < 8:
            yld(t4 + 1, 1 - bi)
        for ti in range(4):
            tt = t4 * 4 + ti
            p.tagctx = f"s1_{tt}"
            for og in range(2):
                bank, bk = psb[og], PS[og]
                for fc in range(8):
                    mm(bank[:], ysl[bi][:, fc * 512 + ti * 128:fc * 512 + (ti + 1) * 128],
                       wo[:, fc * 1024 + og * 512:fc * 1024 + (og + 1) * 512],
                       (f"ysl{bi}", "wo"), bk, start=(fc == 0), stop=False)
                mm(bank[:], onesb[:], boutb[0:1, og * 512:(og + 1) * 512],
                   (CONST, "boutb"), bk, start=False, stop=True, tag="bout")
                dve(lambda e, _b=bank, _og=og: e.tensor_copy(
                    s1st[:, _og * 512:(_og + 1) * 512], _b[:]), (bk,), ("s1st", bk))
            p.dma("pool", s1_hbm[tt * 128:(tt + 1) * 128, :], s1st[:],
                  reads=("s1st",), writes=("s1_hbm",))
    A32.pop(); A16.pop()
    p.barrier()
    if upto < 12:
        p.finalize_and_emit()
        return nc

    p.tagctx = "w2"
    A32.push(); A16.push()
    hbig = [A16.take(16384), A16.take(16384)]
    w2sl = [A16.take(2048), A16.take(2048)]
    b2cs = A32.take(8)
    g1f = A32.take(8 * 512)
    g1tm = A32.take(1024)
    xs1 = A32.take(2048)
    x2st = A32.take(1024)
    p.dma("sp", b2cs[:], hb["b2c"], writes=("b2c",))

    def hload(tg, hbi):
        t0 = tg * 512
        for h4 in range(8):
            p.dma("sp", hbig[hbi][:, h4 * 2048:(h4 + 1) * 2048]
                  .rearrange("p (c t) -> p c t", c=4),
                  h_hbm[h4 * 4 * 128:(h4 * 4 + 4) * 128, t0:t0 + 512]
                  .rearrange("(c p) t -> p c t", p=128), writes=(f"hb{hbi}",))

    hload(0, 0)
    for tg in range(8):
        p.tagctx = f"w2t{tg}"
        t0 = tg * 512
        hbi = tg % 2
        kh = f"hb{hbi}"
        if tg + 1 < 8:
            hload(tg + 1, 1 - hbi)
        for op2 in range(2):
            for h4 in range(8):
                bi = h4 % 2
                kw = f"w2{bi}"
                hh0 = h4 * 4
                p.dma("pool", w2sl[bi][:].rearrange("p (c t) -> p c t", c=4),
                      hb["W2T"][:, hh0 * 1024:(hh0 + 4) * 1024]
                      .rearrange("p (c t) -> p c t", t=1024)[:, :, op2 * 512:(op2 + 1) * 512],
                      writes=(kw,))
                for ci in range(4):
                    for oc4 in range(4):
                        mm(psb[oc4][:],
                           w2sl[bi][:, ci * 512 + oc4 * 128:ci * 512 + (oc4 + 1) * 128],
                           hbig[hbi][:, h4 * 2048 + ci * 512:h4 * 2048 + (ci + 1) * 512],
                           (kh, kw), PS[oc4], start=(h4 == 0 and ci == 0), stop=False)
            for oc4 in range(4):
                oc = op2 * 4 + oc4
                mm(psb[oc4][:], e2b[0:16, oc * 128:(oc + 1) * 128],
                   c16[0:16, t0:t0 + 512], ("c16",), PS[oc4],
                   start=False, stop=True, tag="lora2")
                act(lambda e, _o=oc, _b=psb[oc4]: e.activation(
                    g1f[:, _o * 512:(_o + 1) * 512], _b[:], AF.Silu,
                    bias=b2cs[:, _o:_o + 1]), (PS[oc4], "b2c"), ("g1f", PS[oc4]))
        for ti in range(4):
            tt = tg * 4 + ti
            for oc in range(8):
                pk, pb_ = PS[4 + (oc % 2)], psb[4 + (oc % 2)]
                tr(pb_[:, 0:128], g1f[:, oc * 512 + ti * 128:oc * 512 + (ti + 1) * 128],
                   ident[:], ("g1f", CONST), pk)
                dve(lambda e, _o=oc, _pb=pb_: e.tensor_copy(
                    g1tm[:, _o * 128:(_o + 1) * 128], _pb[:, 0:128]), (pk,), ("g1tm", pk))
            p.dma("sp", xs1[:, 0:1024], x1_hbm[tt * 128:(tt + 1) * 128, :], writes=("xs1",))
            p.dma("sp", xs1[:, 1024:2048], s1_hbm[tt * 128:(tt + 1) * 128, :], writes=("xs1",))
            dve(lambda e: e.tensor_tensor(x2st[:], xs1[:, 1024:2048], g1tm[:], ALU.mult),
                ("xs1", "g1tm"), ("x2st",))
            dve(lambda e: e.tensor_tensor(x2st[:], x2st[:], xs1[:, 0:1024], ALU.add),
                ("x2st", "xs1"), ("x2st",))
            p.dma("pool", out_t[tt * 128:(tt + 1) * 128, :], x2st[:],
                  reads=("x2st",), writes=("out",))
    A32.pop(); A16.pop()
    A32.pop(); A16.pop()
    p.barrier()
    p.finalize_and_emit()
    return nc


_BUILD_CACHE = {}


def _get_program(upto=99, dbg=()):
    key = (upto, tuple(sorted(dbg)))
    if key not in _BUILD_CACHE:
        _BUILD_CACHE[key] = build_program(upto=upto, dbg=dbg)
    return _BUILD_CACHE[key]


def kernel(**inputs):
    consts = _host_constants()
    w = _prep_weights(inputs)
    nc = _get_program()
    x_full = np.asarray(inputs["L0_attn_u"], dtype=np.float32)
    base = {}
    for k, v in consts.items():
        base[k] = v
    for k, v in w.items():
        if k in _W_BF16:
            v = np.asarray(v).astype(bf16)
        base[k] = np.ascontiguousarray(v)
    in_maps = []
    for core in range(8):
        m = dict(base)
        m["x"] = np.ascontiguousarray(x_full[core])
        in_maps.append(m)
    res = run_bass_kernel_spmd(nc, in_maps, core_ids=list(range(8)))
    out = np.stack([np.asarray(res.results[i]["out"]) for i in range(8)], axis=0)
    return out.astype(np.float32)
